# version 28
# baseline (speedup 1.0000x reference)
import contextlib
import numpy as np
import ml_dtypes
import concourse.bass as bass
import concourse.mybir as mybir
from concourse.bass_utils import run_bass_kernel_spmd
from concourse.aot_env import allow_ici_all_to_all

F32 = mybir.dt.float32
F32R = mybir.dt.float32r
BF16 = mybir.dt.bfloat16
U32 = mybir.dt.uint32
AF = mybir.ActivationFunctionType
ALU = mybir.AluOpType
AX = mybir.AxisListType

NCORE = 8
D = 1024
NST = 16
TLOC = 2048
TTOT = NCORE * TLOC
NA = 16
TB = 2048
NB = 8
NTIL = 16
E = 8
CAP = 640
SLOT = 2 * CAP
MAPN = 10368
MAPW = 8
FF = 4096
EPS = 1e-5

DEBUG = False


def build_program():
    nc = bass.Bass("TRN2", target_bir_lowering=False)

    x_in = nc.dram_tensor("x", [TLOC, D], F32, kind="ExternalInput").ap()
    wdt_in = nc.dram_tensor("wdt", [D, D], F32, kind="ExternalInput").ap()
    bdt_in = nc.dram_tensor("bdt", [128, 8], F32, kind="ExternalInput").ap()
    wbc_in = nc.dram_tensor("wbc", [32, D], F32, kind="ExternalInput").ap()
    adiag_in = nc.dram_tensor("adiag", [128, 2048], F32, kind="ExternalInput").ap()
    sel01_in = nc.dram_tensor("sel01", [128, 2048], F32, kind="ExternalInput").ap()
    selbc_in = nc.dram_tensor("selbc", [32, 256], F32, kind="ExternalInput").ap()
    selr_in = nc.dram_tensor("selr", [128, 8], F32, kind="ExternalInput").ap()
    id32_in = nc.dram_tensor("id32", [128, 128], F32, kind="ExternalInput").ap()
    idbf_in = nc.dram_tensor("idbf", [128, 128], BF16, kind="ExternalInput").ap()
    drep_in = nc.dram_tensor("drep", [128, D], F32, kind="ExternalInput").ap()
    wr_in = nc.dram_tensor("wr", [D, E], F32, kind="ExternalInput").ap()
    wi_in = nc.dram_tensor("wi", [D, FF], F32, kind="ExternalInput").ap()
    wo_in = nc.dram_tensor("wo", [FF, D], F32, kind="ExternalInput").ap()
    offsel_in = nc.dram_tensor("offsel", [128, 16], F32, kind="ExternalInput").ap()
    tokid_in = nc.dram_tensor("tokid", [128, 16], F32, kind="ExternalInput").ap()
    gofs_in = nc.dram_tensor("gofs", [128, 1], F32, kind="ExternalInput").ap()
    mapidx_in = nc.dram_tensor("mapidx", [128, 10], U32, kind="ExternalInput").ap()
    iotaE_in = nc.dram_tensor("iotaE", [128, 8], F32, kind="ExternalInput").ap()
    out_t = nc.dram_tensor("out", [TLOC, D], F32, kind="ExternalOutput").ap()
    dbg = {}
    if DEBUG:
        dbg["dt"] = nc.dram_tensor("dbg_dt", [D, TLOC], F32, kind="ExternalOutput").ap()
        dbg["dtx"] = nc.dram_tensor("dbg_dtx", [D, TLOC], F32, kind="ExternalOutput").ap()
        dbg["y"] = nc.dram_tensor("dbg_y", [D, TLOC], F32, kind="ExternalOutput").ap()
        dbg["xln"] = nc.dram_tensor("dbg_xln", [TLOC, D], F32, kind="ExternalOutput").ap()
        dbg["map"] = nc.dram_tensor("dbg_map", [MAPN, MAPW], F32, kind="ExternalOutput").ap()
        dbg["rt"] = nc.dram_tensor("dbg_rt", [128, 6 * 16], F32, kind="ExternalOutput").ap()
        dbg["bpc"] = nc.dram_tensor("dbg_bpc", [NCORE * 32, TLOC], F32, kind="ExternalOutput").ap()
        dbg["hc"] = nc.dram_tensor("dbg_hc", [128, 16], F32, kind="ExternalOutput").ap()
        dbg["ys"] = nc.dram_tensor("dbg_ys", [D, TLOC], F32, kind="ExternalOutput").ap()
        for nm in ["sa", "s01", "bb", "cb", "dA", "dBx", "h",
                   "dA31", "dBx31", "dxb31", "bb3"]:
            dbg[nm] = nc.dram_tensor("dbg_" + nm, [128, 2048], F32, kind="ExternalOutput").ap()
        dbg["ytm"] = nc.dram_tensor("dbg_ytm", [128, 1024], F32, kind="ExternalOutput").ap()
        dbg["r1"] = nc.dram_tensor("dbg_r1", [128, 1024], F32, kind="ExternalOutput").ap()
        dbg["bna"] = nc.dram_tensor("dbg_bna", [128, 2], F32, kind="ExternalOutput").ap()
        dbg["yfl"] = nc.dram_tensor("dbg_yfl", [128, 1024], F32, kind="ExternalOutput").ap()
        dbg["ms"] = nc.dram_tensor("dbg_ms", [128, 4], F32, kind="ExternalOutput").ap()
        dbg["xl0"] = nc.dram_tensor("dbg_xl0", [128, 1024], F32, kind="ExternalOutput").ap()
        dbg["mfm"] = nc.dram_tensor("dbg_mfm", [16, TLOC], F32, kind="ExternalOutput").ap()
        dbg["lcf"] = nc.dram_tensor("dbg_lcf", [16, TLOC], F32, kind="ExternalOutput").ap()
        dbg["o16"] = nc.dram_tensor("dbg_o16", [16, 1], F32, kind="ExternalOutput").ap()
        dbg["pt1"] = nc.dram_tensor("dbg_pt1", [128, 16], F32, kind="ExternalOutput").ap()
        dbg["pc1"] = nc.dram_tensor("dbg_pc1", [128, 24], F32, kind="ExternalOutput").ap()
        dbg["mk1"] = nc.dram_tensor("dbg_mk1", [128, 16], F32, kind="ExternalOutput").ap()

    xfm32_d = nc.dram_tensor("xfm32_d", [D, TLOC], F32)
    a2adt_i = nc.dram_tensor("a2adt_i", [D, TLOC], F32)
    a2adt_o = nc.dram_tensor("a2adt_o", [D, TLOC], F32)
    a2adx_i = nc.dram_tensor("a2adx_i", [D, TLOC], F32)
    a2adx_o = nc.dram_tensor("a2adx_o", [D, TLOC], F32)
    agbpc_i = nc.dram_tensor("agbpc_i", [32, TLOC], F32)
    agbpc_o = nc.dram_tensor("agbpc_o", [NCORE * 32, TLOC], F32, addr_space="Shared")
    a2ay_i = nc.dram_tensor("a2ay_i", [D, TLOC], F32)
    a2ay_o = nc.dram_tensor("a2ay_o", [D, TLOC], F32)
    agxln_i = nc.dram_tensor("agxln_i", [TLOC, D], BF16)
    agxln_o = nc.dram_tensor("agxln_o", [TTOT, D], BF16, addr_space="Shared")
    agcnt_i = nc.dram_tensor("agcnt_i", [16, 8], F32)
    agcnt_o = nc.dram_tensor("agcnt_o", [128, 8], F32)
    map_i = nc.dram_tensor("map_i", [MAPN, MAPW], F32)
    map_o = nc.dram_tensor("map_o", [MAPN, MAPW], F32)
    ageo_i = nc.dram_tensor("ageo_i", [SLOT, D], BF16)
    ageo_o = nc.dram_tensor("ageo_o", [E * SLOT, D], BF16, addr_space="Shared")
    xln32_d = nc.dram_tensor("xln32_d", [TLOC, D], F32)
    wi_bf_d = nc.dram_tensor("wi_bf_d", [D, FF], BF16)
    wo_bf_d = nc.dram_tensor("wo_bf_d", [FF, D], BF16)

    with contextlib.ExitStack() as stk:
        En = stk.enter_context
        En(allow_ici_all_to_all())

        selAbs = En(nc.sbuf_tensor("selAbs", [128, 2048], F32R))
        sel01 = En(nc.sbuf_tensor("sel01r", [128, 2048], F32R))
        selbcr = En(nc.sbuf_tensor("selbcr", [32, 256], F32R))
        selrb = En(nc.sbuf_tensor("selrb", [128, 8], BF16))
        id32 = En(nc.sbuf_tensor("id32s", [128, 128], F32))
        idbf = En(nc.sbuf_tensor("idbfs", [128, 128], BF16))
        d1rep = En(nc.sbuf_tensor("d1rep", [128, D], F32))
        wrsb = En(nc.sbuf_tensor("wrsb", [128, 64], F32))
        bdt = En(nc.sbuf_tensor("bdt_s", [128, 8], F32))
        iotaE = En(nc.sbuf_tensor("iotaE_s", [128, 8], F32))
        tokid = En(nc.sbuf_tensor("tokid_s", [128, 16], F32))
        gofs = En(nc.sbuf_tensor("gofs_s", [128, 1], F32))
        offsel = En(nc.sbuf_tensor("offsel_s", [128, 16], F32))
        mapidx = En(nc.sbuf_tensor("mapidx_s", [128, 10], U32))
        hcar = En(nc.sbuf_tensor("hcar", [128, 17], F32))
        wbcT = En(nc.sbuf_tensor("wbcT", [128, 8 * 32], BF16))
        g0a = En(nc.sbuf_tensor("g0a", [128, 16], F32))
        g1a = En(nc.sbuf_tensor("g1a", [128, 16], F32))
        e0a = En(nc.sbuf_tensor("e0a", [128, 16], F32))
        e1a = En(nc.sbuf_tensor("e1a", [128, 16], F32))
        gi0a = En(nc.sbuf_tensor("gi0a", [128, 16], U32))
        gi1a = En(nc.sbuf_tensor("gi1a", [128, 16], U32))
        cntsb = En(nc.sbuf_tensor("cntsb", [128, 8], F32))
        off16 = En(nc.sbuf_tensor("off16", [16, 1], F32))
        idxu = En(nc.sbuf_tensor("idxu", [128, 10], U32))
        idxf = En(nc.sbuf_tensor("idxf", [128, 80], F32))
        mapz = En(nc.sbuf_tensor("mapz", [128, MAPN * MAPW // 128], F32))
        dtrb = En(nc.sbuf_tensor("dtrb_s", [128, 2048], F32R))
        dtxrb = En(nc.sbuf_tensor("dtxrb_s", [128, 2048], F32R))
        bpcrb = En(nc.sbuf_tensor("bpcrb_s", [32, 2048], F32R))

        ARN = En(nc.sbuf_tensor("ARN", [128, 36864], F32))

        def arn(off, n, dt=F32):
            v = ARN[:, off:off + n]
            if dt != F32:
                v = v.bitcast(dt)
            return v

        xfmbf = arn(0, 8192, BF16)
        xfmbf3 = xfmbf.rearrange("p (dk t) -> p dk t", dk=8)
        wdt32 = arn(8192, 8192)
        xstg = arn(8192, 4096)
        xbfst = arn(12288, 512, BF16)
        st32 = arn(12800, 1024)
        adexp = arn(20480, 2048)
        wdtbf = arn(16384, 4096, BF16)
        dtloc = arn(16384, 2048)
        taux1 = arn(18432, 2048)
        taux2 = arn(20480, 2048)
        xf32ec = arn(22528, 2048)
        bpcsb = arn(24576, 2048)
        wbc32 = arn(26624, 1024)
        dxbc = [arn(31744, 1024), arn(32768, 1024)]
        wbcbf = arn(27648, 512, BF16)
        wdtT = arn(28672, 4096, BF16)

        dtb = arn(0, 2048)
        dtxb = arn(2048, 2048)
        bpcb = arn(4096, 2048)
        bpbc = arn(6144, 2048)
        cbc = arn(8192, 1024, BF16)
        dAb = [arn(9216, 2048), arn(11264, 2048)]
        dBxb = [arn(13312, 2048), arn(15360, 2048), arn(33792, 2048)]
        hhb = [arn(17408, 2048), arn(19456, 2048)]
        hCb = [arn(21504 + 1024 * i, 1024, BF16) for i in range(6)]
        ystg = [arn(27648, 2048), arn(29696, 2048)]

        f_yfl = [arn(0, 1024), arn(1024, 1024)]
        f_ytm = [arn(2048, 1024), arn(3072, 1024)]
        f_x = [arn(4096, 1024), arn(5120, 1024)]
        f_r1 = arn(6144, 1024)
        f_xln = [arn(7168, 1024), arn(8192, 1024)]
        f_xlf = arn(9216, 1024)
        f_xlb = [arn(10240, 512, BF16), arn(10752, 512, BF16)]
        f_bns = arn(11264, 16)
        f_bna = arn(11280, 2)
        f_ms = arn(11282, 4)
        f_lg = arn(11286, 8)
        f_prb = arn(11294, 8)
        f_red = arn(11302, 4)
        f_msk = [arn(11306, 16), arn(11322, 16)]
        f_tp = arn(11338, 8)
        f_tpi = arn(11346, 8, U32)

        lcfm = arn(21504, 2048)
        zer16 = arn(23552, 2048)
        maskfm = arn(25600, 2048)
        postm = arn(31744, 16)
        p_t8 = arn(31760, 8)
        p_c1 = arn(31768, 24)
        p_u = arn(31792, 4, U32)
        tokrep = arn(31796, 8)

        wibf = arn(0, 16384, BF16)
        einfm = arn(16384, 5120, BF16)
        einfm3 = einfm.rearrange("p (dk s) -> p dk s", dk=8)
        hrel = arn(21504, 8192, BF16)
        hrel4 = arn(29696, 4096, BF16)
        wobf = [arn(35840, 512, BF16), arn(36352, 512, BF16)]
        eosb = arn(33792, 2048, BF16)
        eint = arn(33792, 512, BF16)

        cmb_t1 = arn(0, 1024)
        cmb_yf = [arn(1024, 1024), arn(2048, 1024)]
        cmb_xl = [arn(3072, 1024), arn(4096, 1024)]
        cmb_o = [arn(5120, 1024), arn(6144, 1024)]
        cmb_bns = arn(7168, 16)
        cmb_bna = [arn(7184, 2), arn(7186, 2)]
        cmb_ms = [arn(7188, 6), arn(7194, 6)]
        cmb_g0 = [arn(7200, 512, BF16), arn(7712, 512, BF16)]
        cmb_g1 = [arn(8224, 512, BF16), arn(8736, 512, BF16)]

        pbig = En(nc.psum_tensor("pbig", [128, 2048], F32))
        pyA = En(nc.psum_tensor("pyA", [128, 1024], F32))
        pyB = En(nc.psum_tensor("pyB", [128, 1024], F32))
        pbig_bf = pbig.bitcast(BF16)

        semnames = ["sLD", "sGD", "sPE", "sACT", "sVE", "sPO", "sST",
                    "qPE1", "qACT1", "qVE1", "qVE2", "qPO", "qPE2", "qACT2", "qPD", "qDXa", "qDXb",
                    "ccDT", "ccDX", "ccBPC", "ccY", "ccXLN", "ccCNT", "ccMAP", "ccEO"]
        sems = {nm: En(nc.semaphore(nm)) for nm in semnames}
        cnt = {nm: 0 for nm in semnames}

        block = En(nc.Block())
        prog = {k: [] for k in ["sync", "gpsimd", "vector", "scalar", "tensor"]}

        def em(q, fn):
            prog[q].append(fn)

        def tick(nm, k=1):
            cnt[nm] += k
            s = sems[nm]
            def att(instr):
                instr.then_inc(s, k)
            return att

        def wt(q, nm, val=None):
            v = cnt[nm] if val is None else val
            if v <= 0:
                return
            s = sems[nm]
            em(q, lambda e, s=s, v=v: e.wait_ge(s, v))

        def dma(q, dst, src, semnm=None, slow=False):
            semnm = semnm or ("sGD" if q == "gpsimd" else "sLD")
            att = tick(semnm, 16)
            em(q, lambda e, d=dst, s=src, a=att, sl=slow: a(
                e.dma_start(out=d, in_=s, allow_slow_non_contiguous=sl)))
            return cnt[semnm]

        dma("sync", id32[:], id32_in)
        dma("sync", idbf[:], idbf_in)
        dma("sync", wrsb[:].rearrange("p (c e) -> p c e", c=8),
            wr_in.rearrange("(c p) e -> p c e", p=128))
        dma("sync", bdt[:], bdt_in)
        dma("sync", tokid[:], tokid_in)
        dma("sync", gofs[:], gofs_in)
        dma("sync", offsel[:], offsel_in)
        dma("sync", mapidx[:], mapidx_in)
        dma("sync", iotaE[:], iotaE_in)
        dma("sync", adexp, adiag_in)
        dma("sync", d1rep[:], drep_in)
        dma("sync", wdt32.rearrange("p (ee d) -> p ee d", ee=8),
            wdt_in.rearrange("(ee p) d -> p ee d", p=128))
        P0_LD = dma("sync", wbc32[0:32, :], wbc_in)
        dma("gpsimd", selrb[:], selr_in)
        dma("gpsimd", sel01[:], sel01_in)
        P0_GD = dma("gpsimd", selbcr[:], selbc_in)

        wt("vector", "sLD", P0_LD)
        em("vector", lambda v: v.tensor_scalar(d1rep[:], d1rep[:], 1.0, None, op0=ALU.add))
        em("vector", lambda v: v.memset(hcar[:], 0.0))
        em("vector", lambda v: v.memset(zer16[0:16, :], 0.0))
        att = tick("sVE")
        em("vector", lambda v, a=att: a(v.memset(mapz[:], 0.0)))
        VE_INIT = cnt["sVE"]

        wt("scalar", "sLD", P0_LD)
        att = tick("sACT")
        em("scalar", lambda a, t=att: t(a.activation(xf32ec, adexp, AF.Exp)))
        wt("vector", "sACT", cnt["sACT"])
        wt("vector", "sGD", P0_GD)
        att = tick("sVE")
        em("vector", lambda v, t=att: t(v.tensor_tensor(
            xf32ec, xf32ec, sel01[:].bitcast(F32), op=ALU.mult)))
        wt("gpsimd", "sVE", cnt["sVE"])
        GD_SELABS = dma("gpsimd", selAbs[:], xf32ec)

        GD_WIBF = dma("gpsimd", wi_bf_d.ap(), wi_in)
        GD_WOBF = dma("gpsimd", wo_bf_d.ap(), wo_in)

        att = tick("sACT")
        em("scalar", lambda a, t=att: t(a.copy(wdtbf, wdt32)))
        att = tick("sACT")
        em("scalar", lambda a, t=att: t(a.copy(wbcbf[0:32, :], wbc32[0:32, :])))
        ACT_WCAST = cnt["sACT"]

        wt("tensor", "sACT", ACT_WCAST)
        wt("tensor", "sLD", P0_LD)
        prev_evac = 0
        for bi in range(8):
            half = bi % 2
            att = tick("sPE")
            def pe_tr(t, bi=bi, half=half, need=(prev_evac if bi >= 2 else 0), a=att):
                if need:
                    t.wait_ge(sems["sACT"], need)
                last = None
                for dk in range(8):
                    last = t.transpose(
                        pbig_bf[:, half * 2048 + dk * 128: half * 2048 + dk * 128 + 128],
                        wdtbf[:, bi * 1024 + dk * 128: bi * 1024 + dk * 128 + 128],
                        idbf[:])
                a(last)
            em("tensor", pe_tr)
            pe_t = cnt["sPE"]
            att = tick("sACT")
            def act_ev(a, bi=bi, half=half, need=pe_t, t=att):
                a.wait_ge(sems["sPE"], need)
                t(a.copy(wdtT[:, bi * 1024:(bi + 1) * 1024],
                         pbig_bf[:, half * 2048: half * 2048 + 1024]))
            em("scalar", act_ev)
            prev_evac = cnt["sACT"]
        att = tick("sPE")
        def pe_wbc(t, need=prev_evac, a=att):
            t.wait_ge(sems["sACT"], need)
            last = None
            for dk in range(8):
                last = t.transpose(pbig_bf[:, dk * 32: dk * 32 + 32],
                                   wbcbf[0:32, dk * 128:dk * 128 + 128],
                                   idbf[0:32, 0:32])
            a(last)
        em("tensor", pe_wbc)
        pe_t = cnt["sPE"]
        att = tick("sACT")
        def act_wbc(a, need=pe_t, t=att):
            a.wait_ge(sems["sPE"], need)
            t(a.copy(wbcT[:], pbig_bf[:, 0:256]))
        em("scalar", act_wbc)
        W_ACT = cnt["sACT"]
        W_PE = cnt["sPE"]

        PE_A = {}
        ACT_A = {}
        ST_A = {}
        xfm32v = xfm32_d.ap().rearrange("(dk q) t -> q dk t", q=128)
        for a_i in range(NA):
            sl = a_i % 4
            xa = xstg[:, sl * 1024:(sl + 1) * 1024]
            if a_i >= 4:
                wt("sync", "sPE", PE_A[a_i - 4])
            else:
                wt("sync", "sACT", ACT_WCAST)
            ld_t = dma("sync", xa, x_in[a_i * 128:(a_i + 1) * 128, :])
            wt("scalar", "sLD", ld_t)
            if a_i >= 1:
                wt("scalar", "sPE", PE_A[a_i - 1])
            att = tick("sACT")
            em("scalar", lambda a, s=xa, t=att: t(a.copy(xbfst, s)))
            actc = cnt["sACT"]
            att = tick("sPE")
            def pe_a(t, actc=actc, xa=xa, a=att,
                     prev=(ACT_A[a_i - 1] if a_i >= 1 else W_ACT)):
                t.wait_ge(sems["sACT"], actc)
                t.wait_ge(sems["sACT"], prev)
                for dk in range(8):
                    t.transpose(pbig_bf[:, dk * 128: dk * 128 + 128],
                                xbfst[:, dk * 128:dk * 128 + 128], idbf[:])
                last = None
                for dk in range(8):
                    last = t.transpose(pbig[:, 512 + dk * 128: 512 + dk * 128 + 128],
                                       xa[:, dk * 128:dk * 128 + 128], id32[:])
                a(last)
            em("tensor", pe_a)
            PE_A[a_i] = cnt["sPE"]
            wt("scalar", "sPE", PE_A[a_i])
            if a_i >= 1:
                wt("scalar", "sLD", ST_A[a_i - 1])
            att = tick("sACT", 2)
            def act_a(a, a_i=a_i, t=att):
                a.copy(xfmbf3[:, :, a_i * 128:(a_i + 1) * 128],
                       pbig_bf[:, 0:1024].rearrange("p (dk t) -> p dk t", dk=8))
                t(a.copy(st32, pbig[:, 512:1536]))
            em("scalar", act_a)
            ACT_A[a_i] = cnt["sACT"]
            wt("sync", "sACT", ACT_A[a_i])
            ST_A[a_i] = dma("sync", xfm32v[:, :, a_i * 128:(a_i + 1) * 128],
                            st32.rearrange("q (dk t) -> q dk t", dk=8))
        A_LD = cnt["sLD"]

        att = tick("sPE")
        def pe_bpc(t, a=att, need=ACT_A[NA - 1]):
            t.wait_ge(sems["sACT"], need)
            last = None
            for q in range(4):
                for dk in range(8):
                    last = t.matmul(pbig[0:32, q * 512:(q + 1) * 512],
                                    wbcT[:, dk * 32:dk * 32 + 32],
                                    xfmbf3[:, dk, q * 512:(q + 1) * 512],
                                    start=(dk == 0), stop=(dk == 7))
            a(last)
        em("tensor", pe_bpc)
        pe_t = cnt["sPE"]
        wt("scalar", "sPE", pe_t)
        att = tick("sACT")
        em("scalar", lambda a, t=att: t(a.copy(bpcsb[0:32, :], pbig[0:32, :])))
        ACT_BPC = cnt["sACT"]
        wt("sync", "sACT", ACT_BPC)
        BPC_LD = dma("sync", agbpc_i.ap(), bpcsb[0:32, :])
        wt("gpsimd", "sLD", BPC_LD)
        att = tick("ccBPC")
        em("gpsimd", lambda g, t=att: t(g.collective_compute(
            "AllGather", ALU.bypass, replica_groups=[list(range(NCORE))],
            ins=[agbpc_i.ap().opt()], outs=[agbpc_o.ap().opt()])))

        VE_B = {}
        ST_B = {}
        for ec in range(8):
            att = tick("sPE")
            def pe_dt(t, ec=ec, a=att, na=ACT_A[NA - 1],
                      nv=(VE_B[ec - 1] if ec >= 1 else 0), bp=ACT_BPC,
                      pa=cnt["sACT"]):
                t.wait_ge(sems["sACT"], na)
                t.wait_ge(sems["sACT"], bp)
                t.wait_ge(sems["sACT"], pa)
                if nv:
                    t.wait_ge(sems["sVE"], nv)
                last = None
                for q in range(4):
                    for dk in range(8):
                        last = t.matmul(pbig[:, q * 512:(q + 1) * 512],
                                        wdtT[:, (ec * 8 + dk) * 128:(ec * 8 + dk) * 128 + 128],
                                        xfmbf3[:, dk, q * 512:(q + 1) * 512],
                                        start=(dk == 0), stop=(dk == 7))
                a(last)
            em("tensor", pe_dt)
            pe_t = cnt["sPE"]
            wt("sync", "sLD", A_LD)
            if ec == 0:
                wt("sync", "sGD", GD_SELABS)
            if ec >= 1:
                wt("sync", "sVE", VE_B[ec - 1])
            xf_t = dma("sync", xf32ec, xfm32_d.ap()[ec * 128:(ec + 1) * 128, :])
            wt("scalar", "sPE", pe_t)
            if ec == 0:
                wt("scalar", "sGD", GD_SELABS)
            if ec >= 1:
                wt("scalar", "sLD", ST_B[ec - 1])
            att = tick("sACT")
            def act_sp(a, ec=ec, t=att):
                a.activation(taux1, pbig[:, :], AF.Abs, bias=bdt[:, ec:ec + 1])
                a.activation(taux1, taux1, AF.Exp, scale=-1.0)
                t(a.activation(taux1, taux1, AF.Ln, bias=1.0))
            em("scalar", act_sp)
            act_t = cnt["sACT"]
            wt("vector", "sPE", pe_t)
            wt("vector", "sACT", act_t)
            wt("vector", "sLD", xf_t)
            if ec >= 1:
                wt("vector", "sLD", ST_B[ec - 1])
            att = tick("sVE")
            def ve_dt(v, ec=ec, t=att):
                v.tensor_scalar(taux2, pbig[:, :], bdt[:, ec:ec + 1], 0.0,
                                op0=ALU.add, op1=ALU.max)
                v.tensor_tensor(dtloc, taux1, taux2, op=ALU.add)
                t(v.tensor_tensor(taux2, dtloc, xf32ec, op=ALU.mult))
            em("vector", ve_dt)
            VE_B[ec] = cnt["sVE"]
            wt("sync", "sVE", VE_B[ec])
            dma("sync", a2adt_i.ap()[ec * 128:(ec + 1) * 128, :], dtloc)
            ST_B[ec] = dma("sync", a2adx_i.ap()[ec * 128:(ec + 1) * 128, :], taux2)
        B_LD = cnt["sLD"]

        wt("gpsimd", "sLD", B_LD)
        att = tick("ccDT")
        em("gpsimd", lambda g, t=att: t(g.collective_compute(
            "AllToAll", ALU.bypass, replica_groups=[list(range(NCORE))],
            ins=[a2adt_i.ap().opt()], outs=[a2adt_o.ap().opt()])))
        att = tick("ccDX")
        em("gpsimd", lambda g, t=att: t(g.collective_compute(
            "AllToAll", ALU.bypass, replica_groups=[list(range(NCORE))],
            ins=[a2adx_i.ap().opt()], outs=[a2adx_o.ap().opt()])))

        if DEBUG:
            wt("sync", "ccDT", 1)
            wt("sync", "ccDX", 1)
            dma("sync", dbg["dt"], a2adt_o.ap())
            dma("sync", dbg["dtx"], a2adx_o.ap())

        wt("sync", "ccDT", 1)
        wt("sync", "ccDX", 1)
        wt("sync", "ccBPC", 1)
        wt("sync", "sVE", cnt["sVE"])
        wt("sync", "sGD", cnt["sGD"])

        B_VE_FINAL = cnt["sVE"]
        B_ACT_FINAL = cnt["sACT"]
        pend_pe2 = []
        pend_act2 = []
        LDS = {}
        GDS = {}
        ACT_BC = {}
        YLD = {}
        ACT2_grp = {}
        pe1_u = 0
        for b in range(NB):
            if b >= 1:
                wt("sync", "sGD", GDS[b - 1])
                wt("sync", "qPO", b * NTIL)
                wt("sync", "qVE1", 2 * b * NTIL)
                wt("sync", "qPD", 2 * b * NTIL)
            dma("sync", dtb, a2adt_o.ap()[b * 128:(b + 1) * 128, :])
            dma("sync", dtxb, a2adx_o.ap()[b * 128:(b + 1) * 128, :])
            LDS[b] = dma("sync", bpcb[0:32, :], agbpc_o.ap()[b * 32:(b + 1) * 32, :])
            wt("gpsimd", "sLD", LDS[b])
            if b >= 1:
                wt("gpsimd", "qPE1", 4 * b * NTIL)
            dma("gpsimd", dtrb[:], dtb)
            dma("gpsimd", dtxrb[:], dtxb)
            GDS[b] = dma("gpsimd", bpcrb[:], bpcb[0:32, :])
            att = tick("sPE")
            def pe_bc1(t, b=b, a=att, gd=GDS[b]):
                t.wait_ge(sems["sGD"], gd)
                if b >= 1:
                    t.wait_ge(sems["qACT1"], 4 * b * NTIL)
                    t.wait_ge(sems["qVE1"], 2 * b * NTIL)
                else:
                    t.wait_ge(sems["sVE"], B_VE_FINAL)
                    t.wait_ge(sems["sACT"], B_ACT_FINAL)
                last = None
                for ch in range(4):
                    last = t.matmul(pbig[:, ch * 512:(ch + 1) * 512],
                                    selbcr[:, 0:128],
                                    bpcrb[0:32, ch * 512:(ch + 1) * 512],
                                    start=True, stop=True)
                a(last)
            em("tensor", pe_bc1)
            pe_t = cnt["sPE"]
            wt("scalar", "sPE", pe_t)
            att = tick("sACT")
            em("scalar", lambda a, t=att: t(a.copy(bpbc, pbig[:, :])))
            act_t = cnt["sACT"]
            att = tick("sPE")
            def pe_bc2(t, a=att, need=act_t):
                t.wait_ge(sems["sACT"], need)
                last = None
                for ch in range(4):
                    last = t.matmul(pbig[:, ch * 512:(ch + 1) * 512],
                                    selbcr[:, 128:256],
                                    bpcrb[0:32, ch * 512:(ch + 1) * 512],
                                    start=True, stop=True)
                a(last)
            em("tensor", pe_bc2)
            pe_t = cnt["sPE"]
            wt("scalar", "sPE", pe_t)
            att = tick("sACT")
            em("scalar", lambda a, t=att: t(a.copy(cbc, pbig[:, :])))
            ACT_BC[b] = cnt["sACT"]

            def emit_pd(b=b, k2=0):
                g2 = b * NTIL + k2
                for q2 in range(2):
                    att = tick("qPD")
                    def pd(gp, par3=g2 % 3, g=g2, q=q2, t=att, bc=ACT_BC[b]):
                        gp.wait_ge(sems["qDXa"], 2 * g + q + 1)
                        gp.wait_ge(sems["sACT"], bc)
                        if q == 0 and g >= 3:
                            gp.wait_ge(sems["qVE2"], g - 1)
                        t(gp.tensor_tensor(
                            dBxb[par3][:, q * 512:(q + 1) * 512],
                            dxbc[g % 2][:, q * 512:(q + 1) * 512],
                            bpbc[:, q * 512:(q + 1) * 512], op=ALU.mult))
                    em("gpsimd", pd)

            emit_pd(b, 0)

            for k in range(NTIL):
                g = b * NTIL + k
                par = g % 2
                grp = k // 4
                jj = k % 4
                gid = b * 4 + grp
                while pend_pe2 and len(pend_pe2) >= 4 and pend_pe2[3][0] is not None \
                        and pend_pe2[3][0] <= g - 2:
                    for _ in range(4):
                        em("tensor", pend_pe2.pop(0)[1])
                while pend_act2 and pend_act2[0][0] <= g - 2:
                    em("scalar", pend_act2.pop(0)[1])
                if k + 1 < NTIL:
                    emit_pd(b, k + 1)
                for q in range(4):
                    u = 4 * g + q
                    sl = u % 2
                    att = tick("qPE1")
                    def pe1(t, k=k, q=q, u=u, sl=sl, g=g, a=att, bc=ACT_BC[b]):
                        t.wait_ge(sems["sACT"], bc)
                        if u >= 2:
                            t.wait_ge(sems["qACT1"], u - 1)
                        if q < 2 and g >= 1:
                            t.wait_ge(sems["qVE1"], 2 * g + q - 1)
                        if q >= 2:
                            t.wait_ge(sems["qDXa"], 2 * g + q - 1)
                        c0 = q * 512
                        mm = t.matmul(pbig[:, sl * 512:(sl + 1) * 512],
                                      selAbs[:, k * 128:(k + 1) * 128],
                                      dtrb[:, c0:c0 + 512], start=True, stop=True)
                        mm = t.matmul(
                            pbig[:, 1024 + sl * 512:1024 + (sl + 1) * 512],
                            sel01[:, k * 128:(k + 1) * 128],
                            dtxrb[:, c0:c0 + 512], start=True, stop=True)
                        a(mm)
                    em("tensor", pe1)
                    att = tick("qACT1")
                    def act1(a, par=par, q=q, u=u, sl=sl, g=g, t=att):
                        a.wait_ge(sems["qPE1"], u + 1)
                        if g >= 2 and q == 0:
                            a.wait_ge(sems["qVE2"], g - 1)
                            a.wait_ge(sems["qVE1"], 2 * g)
                        t(a.activation(dAb[par][:, q * 512:(q + 1) * 512],
                                       pbig[:, sl * 512:(sl + 1) * 512],
                                       AF.Exp, scale=-1.0))
                    em("scalar", act1)
                    if q < 2:
                        att = tick("qDXa")
                        def evq(a, g=g, q=q, sl=sl, t=att):
                            if g >= 2 and q == 0:
                                a.wait_ge(sems["qPD"], 2 * g - 2)
                            t(a.copy(dxbc[g % 2][:, q * 512:(q + 1) * 512],
                                     pbig[:, 1024 + sl * 512:1024 + (sl + 1) * 512]))
                        em("scalar", evq)
                    if q >= 2:
                        att = tick("qVE1")
                        def ve1(v, par3=g % 3, q=q, u=u, sl=sl, t=att):
                            v.wait_ge(sems["qPE1"], u + 1)
                            t(v.tensor_tensor(
                                dBxb[par3][:, q * 512:(q + 1) * 512],
                                pbig[:, 1024 + sl * 512:1024 + (sl + 1) * 512],
                                bpbc[:, q * 512:(q + 1) * 512],
                                op=ALU.mult))
                        em("vector", ve1)
                att = tick("qVE2")
                init0 = (b % 4 == 0)
                prev_copy = None
                if g >= 1:
                    bp, kp = (g - 1) // NTIL, (g - 1) % NTIL
                    if bp % 4 != 3:
                        prev_copy = (kp, (g - 1) % 2)
                def ve2(v, par=par, par3=g % 3, k=k, g=g, init0=init0, t=att,
                        pc=prev_copy):
                    v.wait_ge(sems["qACT1"], 4 * g + 4)
                    v.wait_ge(sems["qPD"], 2 * g + 2)
                    if g >= 2:
                        v.wait_ge(sems["qPO"], g - 1)
                    ini = 0.0 if init0 else hcar[:, k:k + 1]
                    v.tensor_tensor_scan(hhb[par], dAb[par], dBxb[par3], ini,
                                         op0=ALU.mult, op1=ALU.add)
                    if pc is not None:
                        t(v.tensor_copy(hcar[:, pc[0]:pc[0] + 1],
                                        hhb[pc[1]][:, 2047:2048]))
                    else:
                        t(v.tensor_copy(hcar[:, 16:17], dAb[par][:, 0:1]))
                em("vector", ve2)
                if DEBUG and b == 0 and k == 0:
                    wt("sync", "sGD", GD_SELABS)
                    dma("sync", dbg["sa"], selAbs[:].bitcast(F32), semnm="sST")
                    dma("sync", dbg["s01"], sel01[:].bitcast(F32), semnm="sST")
                    wt("sync", "sACT", ACT_BC[0])
                    dma("sync", dbg["bb"], bpbc, semnm="sST")
                    dma("gpsimd", dbg["cb"], cbc, semnm="sST")
                    wt("sync", "qACT1", 4)
                    wt("sync", "qVE1", 2)
                    wt("sync", "qPD", 2)
                    dma("sync", dbg["dA"], dAb[0], semnm="sST")
                    dma("sync", dbg["dBx"], dBxb[0], semnm="sST")
                    wt("sync", "qVE2", 1)
                    dma("sync", dbg["h"], hhb[0], semnm="sST")
                    DBG_STALL = cnt["sST"]
                if DEBUG and b == 3 and k == 1:
                    wt("sync", "qACT1", 4 * 49 + 4)
                    wt("sync", "qVE1", 2 * 49 + 2)
                    wt("sync", "qPD", 50)
                    dma("sync", dbg["dA31"], dAb[1], semnm="sST")
                    dma("sync", dbg["dBx31"], dBxb[1], semnm="sST")
                    dma("sync", dbg["dxb31"][:, 0:1024], dxbc[1], semnm="sST")
                    dma("sync", dbg["bb3"], bpbc, semnm="sST")
                att = tick("qPO")
                def po(gp, par=par, g=g, b=b, t=att, bc=ACT_BC[b]):
                    gp.wait_ge(sems["qVE2"], g + 1)
                    gp.wait_ge(sems["sACT"], bc)
                    if g >= 6:
                        gp.wait_ge(sems["qPE2"], g - 5)
                    t(gp.tensor_tensor(hCb[g % 6], hhb[par], cbc, op=ALU.mult))
                em("gpsimd", po)
                att = tick("qPE2")
                def pe2(t, jj=jj, g=g, gid=gid, a=att):
                    t.wait_ge(sems["qPO"], g + 1)
                    if jj == 0 and gid >= 1:
                        t.wait_ge(sems["qACT2"], gid)
                    last = None
                    for q in range(2):
                        t.matmul(pyA[32 * jj:32 * jj + 8, q * 512:(q + 1) * 512],
                                 selrb[:], hCb[g % 6][:, q * 512:(q + 1) * 512],
                                 start=True, stop=True, tile_position=(0, 32 * jj))
                        last = t.matmul(
                            pyB[32 * jj:32 * jj + 8, q * 512:(q + 1) * 512],
                            selrb[:], hCb[g % 6][:, 1024 + q * 512:1024 + (q + 1) * 512],
                            start=True, stop=True, tile_position=(0, 32 * jj))
                    a(last)
                pend_pe2.append((g if jj == 3 else None, pe2))
                if jj == 3:
                    att = tick("qACT2")
                    def act2(a, g=g, gid=gid, t=att,
                             yld=(YLD[gid - 2] if gid >= 2 else 0)):
                        a.wait_ge(sems["qPE2"], g + 1)
                        if yld:
                            a.wait_ge(sems["sLD"], yld)
                        gpar = gid % 2
                        a.copy(ystg[gpar][:, 0:1024], pyA[:, :])
                        t(a.copy(ystg[gpar][:, 1024:2048], pyB[:, :]))
                    pend_act2.append((g, act2))
                    if k == NTIL - 1:
                        for _, fn in pend_pe2:
                            em("tensor", fn)
                        pend_pe2.clear()
                        for _, fn in pend_act2:
                            em("scalar", fn)
                        pend_act2.clear()
                    wt("sync", "qACT2", cnt["qACT2"])
                    gpar = gid % 2
                    for j4 in range(4):
                        krow = grp * 4 + j4
                        dma("sync",
                            a2ay_i.ap()[b * 128 + krow * 8: b * 128 + krow * 8 + 8, :],
                            ystg[gpar][32 * j4:32 * j4 + 8, :])
                    YLD[gid] = cnt["sLD"]

        for _, fn in pend_pe2:
            em("tensor", fn)
        for _, fn in pend_act2:
            em("scalar", fn)
        pend_pe2 = []
        pend_act2 = []
        wt("gpsimd", "sLD", cnt["sLD"])
        att = tick("ccY")
        em("gpsimd", lambda g, t=att: t(g.collective_compute(
            "AllToAll", ALU.bypass, replica_groups=[list(range(NCORE))],
            ins=[a2ay_i.ap().opt()], outs=[a2ay_o.ap().opt()])))

        if DEBUG:
            wt("sync", "ccY", 1)
            dma("sync", dbg["y"], a2ay_o.ap())
            dma("sync", dbg["ys"], a2ay_i.ap())
            dma("sync", dbg["bpc"], agbpc_o.ap())
            wt("sync", "qVE2", NB * NTIL)
            dma("sync", dbg["hc"], hcar[:, 0:16])

        SCAN_VE = cnt["qVE2"]
        wt("sync", "ccY", 1)
        wt("sync", "qVE2", SCAN_VE)
        wt("sync", "qPO", cnt["qPO"])
        wt("tensor", "ccY", 1)
        yv = a2ay_o.ap().rearrange("(dk q) t -> q dk t", q=128)
        T1c = {}; T2c = {}; T3c = {}; T4c = {}
        A1c = {}; A2c = {}; A3c = {}; A4c = {}; A5c = {}; A6c = {}
        VAc = {}; VBc = {}; VCc = {}; VDc = {}
        S1c = {}; S2c = {}; LDc = {}
        for r in range(NA + 2):
            j = r
            if j < NA:
                p2 = j % 2
                if j >= 2:
                    wt("sync", "sPE", T1c[j - 2])
                    wt("sync", "sVE", VAc[j - 2])
                dma("sync", f_yfl[p2].rearrange("q (dk t) -> q dk t", dk=8),
                    yv[:, :, j * 128:(j + 1) * 128])
                LDc[j] = dma("sync", f_x[p2], x_in[j * 128:(j + 1) * 128, :])
                att = tick("sPE")
                def t1(t, j=j, p2=p2, a=att, ld=LDc[j],
                       af=(A1c.get(j - 1, 0))):
                    t.wait_ge(sems["sLD"], ld)
                    if af:
                        t.wait_ge(sems["sACT"], af)
                    if j == 0:
                        t.wait_ge(sems["qACT1"], 4 * NB * NTIL)
                        t.wait_ge(sems["qVE1"], 2 * NB * NTIL)
                    last = None
                    for dk in range(8):
                        last = t.transpose(pbig[:, dk * 128:dk * 128 + 128],
                                           f_yfl[p2][:, dk * 128:dk * 128 + 128],
                                           id32[:])
                    a(last)
                em("tensor", t1)
                T1c[j] = cnt["sPE"]
                att = tick("sACT")
                def a1(a, j=j, p2=p2, t=att, pe=T1c[j], vf=(VAc.get(j - 2, 0))):
                    a.wait_ge(sems["sPE"], pe)
                    if vf:
                        a.wait_ge(sems["sVE"], vf)
                    t(a.copy(f_ytm[p2], pbig[:, 0:1024]))
                em("scalar", a1)
                A1c[j] = cnt["sACT"]
                att = tick("sVE")
                def v1(v, j=j, p2=p2, t=att, ac=A1c[j], ld=LDc[j]):
                    v.wait_ge(sems["sACT"], ac)
                    v.wait_ge(sems["sLD"], ld)
                    v.tensor_tensor(f_r1, f_x[p2], d1rep[:], op=ALU.mult)
                    v.tensor_tensor(f_r1, f_r1, f_ytm[p2], op=ALU.add)
                    v.drain()
                    v.bn_stats(f_bns[:, 0:6], f_r1[:, 0:512])
                    v.bn_stats(f_bns[:, 6:12], f_r1[:, 512:1024])
                    v.drain()
                    v.bn_aggr(f_bna, f_bns[:, 0:12])
                    v.drain()
                    v.tensor_scalar(f_ms[:, 2:3], f_bna[:, 1:2], EPS, None, op0=ALU.add)
                    t(v.drain())
                em("vector", v1)
                VAc[j] = cnt["sVE"]
                att = tick("sACT")
                em("scalar", lambda a, t=att, ve=VAc[j]: (
                    a.wait_ge(sems["sVE"], ve),
                    t(a.activation(f_ms[:, 0:1], f_ms[:, 2:3], AF.Sqrt)))[-1])
                A2c[j] = cnt["sACT"]
                att = tick("sVE")
                def v2(v, j=j, p2=p2, t=att, ac=A2c[j],
                       t2f=T2c.get(j - 2, 0), s1f=S1c.get(j - 2, 0)):
                    v.wait_ge(sems["sACT"], ac)
                    if t2f:
                        v.wait_ge(sems["sPE"], t2f)
                    if s1f:
                        v.wait_ge(sems["sLD"], s1f)
                    v.reciprocal(f_ms[:, 1:2], f_ms[:, 0:1])
                    v.drain()
                    t(v.tensor_scalar(f_xln[p2], f_r1, f_bna[:, 0:1], f_ms[:, 1:2],
                                      op0=ALU.subtract, op1=ALU.mult))
                em("vector", v2)
                VBc[j] = cnt["sVE"]
                wt("sync", "sVE", VBc[j])
                S1c[j] = dma("sync", xln32_d.ap()[j * 128:(j + 1) * 128, :], f_xln[p2])
                att = tick("sACT")
                def a3(a, j=j, p2=p2, t=att, ve=VBc[j], s2f=S2c.get(j - 2, 0)):
                    a.wait_ge(sems["sVE"], ve)
                    if s2f:
                        a.wait_ge(sems["sLD"], s2f)
                    t(a.copy(f_xlb[p2], f_xln[p2]))
                em("scalar", a3)
                A3c[j] = cnt["sACT"]
                wt("sync", "sACT", A3c[j])
                S2c[j] = dma("sync", agxln_i.ap()[j * 128:(j + 1) * 128, :], f_xlb[p2])
            j2 = r - 1
            if 0 <= j2 < NA:
                p2 = j2 % 2
                att = tick("sPE")
                def t2(t, j=j2, p2=p2, a=att, ve=VBc[j2], af=A4c.get(j2 - 1, 0)):
                    t.wait_ge(sems["sVE"], ve)
                    if af:
                        t.wait_ge(sems["sACT"], af)
                    last = None
                    for dk in range(8):
                        last = t.transpose(
                            pbig[:, 1024 + dk * 128:1024 + dk * 128 + 128],
                            f_xln[p2][:, dk * 128:dk * 128 + 128], id32[:])
                    a(last)
                em("tensor", t2)
                T2c[j2] = cnt["sPE"]
                att = tick("sACT")
                def a4(a, t=att, pe=T2c[j2], t3f=T3c.get(j2 - 1, 0)):
                    a.wait_ge(sems["sPE"], pe)
                    if t3f:
                        a.wait_ge(sems["sPE"], t3f)
                    t(a.copy(f_xlf, pbig[:, 1024:2048]))
                em("scalar", a4)
                A4c[j2] = cnt["sACT"]
                att = tick("sPE")
                def t3(t, a=att, ac=A4c[j2], vcf=VCc.get(j2 - 1, 0)):
                    t.wait_ge(sems["sACT"], ac)
                    if vcf:
                        t.wait_ge(sems["sVE"], vcf)
                    last = None
                    for dk in range(8):
                        last = t.matmul(pyA[:, 0:8], f_xlf[:, dk * 128:dk * 128 + 128],
                                        wrsb[:, dk * 8:(dk + 1) * 8],
                                        start=(dk == 0), stop=(dk == 7))
                    a(last)
                em("tensor", t3)
                T3c[j2] = cnt["sPE"]
                att = tick("sVE")
                def v3(v, t=att, pe=T3c[j2]):
                    v.wait_ge(sems["sPE"], pe)
                    v.reduce_max(f_red[:, 0:1], pyA[:, 0:8], axis=AX.X)
                    v.drain()
                    t(v.tensor_scalar(f_lg, pyA[:, 0:8], f_red[:, 0:1], None,
                                      op0=ALU.subtract))
                em("vector", v3)
                VCc[j2] = cnt["sVE"]
                att = tick("sACT")
                em("scalar", lambda a, t=att, ve=VCc[j2]: (
                    a.wait_ge(sems["sVE"], ve),
                    t(a.activation(f_lg, f_lg, AF.Exp)))[-1])
                A5c[j2] = cnt["sACT"]
                att = tick("sVE")
                def v4(v, j=j2, t=att, ac=A5c[j2], t4f=T4c.get(j2 - 2, 0)):
                    v.wait_ge(sems["sACT"], ac)
                    if t4f:
                        v.wait_ge(sems["sPE"], t4f)
                    v.reduce_sum(f_red[:, 1:2], f_lg, axis=AX.X)
                    v.drain()
                    v.reciprocal(f_red[:, 2:3], f_red[:, 1:2])
                    v.drain()
                    v.tensor_scalar(f_prb, f_lg, f_red[:, 2:3], None, op0=ALU.mult)
                    v.drain()
                    v.max(f_tp, f_prb)
                    v.drain()
                    v.max_index(f_tpi, f_tp, f_prb)
                    v.drain()
                    v.tensor_copy(g0a[:, j:j + 1], f_tp[:, 0:1])
                    v.tensor_copy(g1a[:, j:j + 1], f_tp[:, 1:2])
                    v.tensor_copy(e0a[:, j:j + 1], f_tpi[:, 0:1])
                    v.tensor_copy(e1a[:, j:j + 1], f_tpi[:, 1:2])
                    v.drain()
                    v.tensor_scalar(f_msk[j % 2][:, 0:8], iotaE[:], e0a[:, j:j + 1],
                                    None, op0=ALU.is_equal)
                    v.tensor_scalar(f_msk[j % 2][:, 8:16], iotaE[:], e1a[:, j:j + 1],
                                    None, op0=ALU.is_equal)
                    t(v.drain())
                em("vector", v4)
                VDc[j2] = cnt["sVE"]
            j4 = r - 2
            if 0 <= j4 < NA:
                att = tick("sPE")
                def t4(t, j=j4, a=att, ve=VDc[j4], af=A6c.get(j4 - 1, 0)):
                    t.wait_ge(sems["sVE"], ve)
                    if af:
                        t.wait_ge(sems["sACT"], af)
                    a(t.transpose(pyA[0:16, 16:144], f_msk[j % 2][:, 0:16], id32[:]))
                em("tensor", t4)
                T4c[j4] = cnt["sPE"]
                att = tick("sACT")
                em("scalar", lambda a, j=j4, t=att, pe=T4c[j4]: (
                    a.wait_ge(sems["sPE"], pe),
                    t(a.copy(maskfm[0:16, j * 128:(j + 1) * 128], pyA[0:16, 16:144])))[-1])
                A6c[j4] = cnt["sACT"]
        F0_ACT = {NA - 1: A6c[NA - 1]}
        AGX_LD = cnt["sLD"]
        wt("gpsimd", "sLD", AGX_LD)
        att = tick("ccXLN")
        em("gpsimd", lambda g, t=att: t(g.collective_compute(
            "AllGather", ALU.bypass, replica_groups=[list(range(NCORE))],
            ins=[agxln_i.ap().opt()], outs=[agxln_o.ap().opt()])))

        if DEBUG:
            dma("sync", dbg["xln"], xln32_d.ap())

        wt("vector", "sACT", F0_ACT[NA - 1])
        att = tick("sVE")
        def ve_f1(v, t=att):
            v.memset(zer16[0:16, :], 0.0)
            v.drain()
            v.tensor_tensor_scan(lcfm[0:16, :], maskfm[0:16, :], zer16[0:16, :],
                                 0.0, op0=ALU.add, op1=ALU.add)
            v.drain()
            t(v.tensor_copy(cntsb[0:16, 0:1], lcfm[0:16, TLOC - 1:TLOC]))
        em("vector", ve_f1)
        ve_t = cnt["sVE"]
        wt("sync", "sVE", ve_t)
        dma("sync", agcnt_i.ap()[:, 0:1], cntsb[0:16, 0:1], slow=True)
        wt("gpsimd", "sLD", cnt["sLD"])
        att = tick("ccCNT")
        em("gpsimd", lambda g, t=att: t(g.collective_compute(
            "AllGather", ALU.bypass, replica_groups=[list(range(NCORE))],
            ins=[agcnt_i.ap().opt()], outs=[agcnt_o.ap().opt()])))
        wt("sync", "ccCNT", 1)
        cnt_ld = dma("sync", cntsb[:], agcnt_o.ap())
        wt("tensor", "sLD", cnt_ld)
        att = tick("sPE")
        em("tensor", lambda t, a=att: a(t.matmul(
            pbig[0:16, 0:1], offsel[:], cntsb[:, 0:1], start=True, stop=True)))
        pe_t = cnt["sPE"]
        wt("scalar", "sPE", pe_t)
        att = tick("sACT")
        em("scalar", lambda a, t=att: t(a.copy(off16[:], pbig[0:16, 0:1])))
        act_t = cnt["sACT"]
        wt("vector", "sACT", act_t)
        att = tick("sVE")
        def ve_pos(v, t=att):
            v.tensor_scalar(off16[:], off16[:], -1.0, None, op0=ALU.add)
            v.drain()
            t(v.tensor_scalar(lcfm[0:16, :], lcfm[0:16, :], off16[:], None,
                              op0=ALU.add))
        em("vector", ve_pos)
        POS_VE = cnt["sVE"]
        if DEBUG:
            wt("sync", "sVE", POS_VE)
            dma("sync", dbg["mfm"], maskfm[0:16, :], semnm="sST")
            dma("sync", dbg["lcf"], lcfm[0:16, :], semnm="sST")
            dma("sync", dbg["o16"], off16[:], semnm="sST")

        wt("gpsimd", "sVE", VE_INIT)
        GD_MAPZ = dma("gpsimd", map_i.ap().rearrange("(p a) w -> p (a w)", p=128),
                      mapz[:])

        F2_ACT = 0
        F2_GD_prev = 0
        for j in range(NA):
            wt("tensor", "sVE", POS_VE)
            if DEBUG and j == 2:
                wt("tensor", "sST", cnt["sST"])
            if F2_ACT:
                wt("tensor", "sACT", F2_ACT)
            att = tick("sPE")
            em("tensor", lambda t, j=j, a=att: a(t.transpose(
                pbig[:, 1536:1536 + 16],
                lcfm[0:16, j * 128:(j + 1) * 128], id32[0:16, 0:16])))
            pe_t = cnt["sPE"]
            wt("scalar", "sPE", pe_t)
            if j >= 1:
                wt("scalar", "sVE", F2_VE_prev)
            att = tick("sACT")
            em("scalar", lambda a, t=att: t(a.copy(postm, pbig[:, 1536:1536 + 16])))
            F2_ACT = cnt["sACT"]
            if DEBUG and j == 1:
                wt("sync", "sACT", F2_ACT)
                dma("sync", dbg["pt1"], postm, semnm="sST")
            wt("vector", "sACT", F2_ACT)
            if F2_GD_prev:
                wt("vector", "sGD", F2_GD_prev)
            att = tick("sVE")
            def ve_f2(v, j=j, t=att):
                v.tensor_scalar(f_msk[0][:, 0:8], iotaE[:], e0a[:, j:j + 1], None,
                                op0=ALU.is_equal)
                v.tensor_scalar(f_msk[0][:, 8:16], iotaE[:], e1a[:, j:j + 1], None,
                                op0=ALU.is_equal)
                v.drain()
                v.tensor_tensor(p_t8, postm[:, 0:8], f_msk[0][:, 0:8], op=ALU.mult)
                v.drain()
                v.reduce_sum(p_c1[:, 0:1], p_t8, axis=AX.X)
                v.drain()
                v.tensor_tensor(p_t8, postm[:, 8:16], f_msk[0][:, 8:16], op=ALU.mult)
                v.drain()
                v.reduce_sum(p_c1[:, 1:2], p_t8, axis=AX.X)
                v.drain()
                v.tensor_scalar(p_c1[:, 2:3], p_c1[:, 0:1], float(CAP) - 0.5, None,
                                op0=ALU.is_lt)
                v.tensor_scalar(p_c1[:, 3:4], p_c1[:, 1:2], float(CAP) - 0.5, None,
                                op0=ALU.is_lt)
                v.drain()
                v.tensor_tensor(g0a[:, j:j + 1], g0a[:, j:j + 1], p_c1[:, 2:3], op=ALU.mult)
                v.tensor_tensor(g1a[:, j:j + 1], g1a[:, j:j + 1], p_c1[:, 3:4], op=ALU.mult)
                v.drain()
                v.tensor_scalar(p_c1[:, 4:5], e0a[:, j:j + 1], float(SLOT), gofs[:, 0:1],
                                op0=ALU.mult, op1=ALU.add)
                v.tensor_scalar(p_c1[:, 5:6], e1a[:, j:j + 1], float(SLOT), gofs[:, 0:1],
                                op0=ALU.mult, op1=ALU.add)
                v.drain()
                v.tensor_tensor(p_c1[:, 4:5], p_c1[:, 4:5], p_c1[:, 0:1], op=ALU.add)
                v.tensor_tensor(p_c1[:, 5:6], p_c1[:, 5:6], p_c1[:, 1:2], op=ALU.add)
                v.drain()
                v.tensor_tensor(p_c1[:, 6:7], p_c1[:, 4:5], p_c1[:, 2:3], op=ALU.mult)
                v.tensor_tensor(p_c1[:, 7:8], p_c1[:, 5:6], p_c1[:, 3:4], op=ALU.mult)
                v.drain()
                v.tensor_copy(gi0a[:, j:j + 1], p_c1[:, 6:7])
                v.tensor_copy(gi1a[:, j:j + 1], p_c1[:, 7:8])
                v.tensor_scalar(p_c1[:, 8:9], p_c1[:, 2:3], -1.0, 1.0,
                                op0=ALU.mult, op1=ALU.add)
                v.tensor_scalar(p_c1[:, 9:10], p_c1[:, 3:4], -1.0, 1.0,
                                op0=ALU.mult, op1=ALU.add)
                v.tensor_scalar(p_c1[:, 10:11], tokid[:, j:j + 1], 30000.0, None,
                                op0=ALU.add)
                v.drain()
                v.tensor_tensor(p_c1[:, 8:9], p_c1[:, 8:9], p_c1[:, 10:11], op=ALU.mult)
                v.tensor_tensor(p_c1[:, 9:10], p_c1[:, 9:10], p_c1[:, 10:11], op=ALU.mult)
                v.drain()
                v.tensor_tensor(p_c1[:, 8:9], p_c1[:, 8:9], p_c1[:, 6:7], op=ALU.add)
                v.tensor_tensor(p_c1[:, 9:10], p_c1[:, 9:10], p_c1[:, 7:8], op=ALU.add)
                v.drain()
                v.tensor_copy(p_u[:, 0:1], p_c1[:, 8:9])
                v.tensor_copy(p_u[:, 1:2], p_c1[:, 9:10])
                v.tensor_scalar(tokrep, iotaE[:], 0.0, tokid[:, j:j + 1],
                                op0=ALU.mult, op1=ALU.add)
                t(v.drain())
            em("vector", ve_f2)
            ve_t = cnt["sVE"]
            F2_VE_prev = cnt["sVE"]
            if DEBUG and j == 1:
                wt("sync", "sVE", ve_t)
                dma("sync", dbg["pc1"], p_c1, semnm="sST")
                dma("sync", dbg["mk1"], f_msk[0], semnm="sST")
                F2_DBG = cnt["sST"]
            wt("gpsimd", "sVE", ve_t)
            if j == 0:
                wt("gpsimd", "sGD", GD_MAPZ)
            for kk in range(2):
                att = tick("sGD", 16)
                em("gpsimd", lambda g, kk=kk, a=att: a(g.indirect_dma_start(
                    out=map_i.ap(),
                    out_offset=bass.IndirectOffsetOnAxis(ap=p_u[:, kk:kk + 1], axis=0),
                    in_=tokrep, in_offset=None,
                    bounds_check=E * SLOT - 1, oob_is_err=False)))
            F2_GD_prev = cnt["sGD"]
        F2_GD = cnt["sGD"]
        wt("gpsimd", "sGD", F2_GD)
        att = tick("ccMAP")
        em("gpsimd", lambda g, t=att: t(g.collective_compute(
            "AllReduce", ALU.add, replica_groups=[list(range(NCORE))],
            ins=[map_i.ap().opt()], outs=[map_o.ap().opt()])))

        if DEBUG:
            wt("sync", "ccMAP", 1)
            dma("sync", dbg["map"], map_o.ap())
            wt("sync", "sVE", cnt["sVE"])
            for col, src in enumerate([g0a, g1a, e0a, e1a]):
                dma("sync", dbg["rt"][:, col * 16:(col + 1) * 16], src[:])
            dma("sync", dbg["rt"][:, 64:80], gi0a[:].bitcast(F32))
            dma("sync", dbg["rt"][:, 80:96], gi1a[:].bitcast(F32))

        wt("sync", "sVE", cnt["sVE"])
        wt("sync", "sGD", GD_WIBF)
        GD_WI_LD = dma("sync", wibf.rearrange("p (dk f) -> p dk f", dk=8),
                       wi_bf_d.ap().rearrange("(dk p) f -> p dk f", p=128))
        wt("gpsimd", "ccMAP", 1)
        for i in range(10):
            att = tick("sGD", 16)
            em("gpsimd", lambda g, i=i, a=att: a(g.indirect_dma_start(
                out=idxf[:, i * 8:(i + 1) * 8], out_offset=None,
                in_=map_o.ap(),
                in_offset=bass.IndirectOffsetOnAxis(ap=mapidx[:, i:i + 1], axis=0))))
        GD_MAPG = cnt["sGD"]
        wt("vector", "sGD", GD_MAPG)
        att = tick("sVE")
        def ve_idx(v, t=att):
            v.tensor_copy(idxu[:], idxf.rearrange("p (i w) -> p i w", w=8)[:, :, 0])
            t(v.drain())
        em("vector", ve_idx)
        VE_IDX = cnt["sVE"]
        wt("gpsimd", "sVE", VE_IDX)
        wt("gpsimd", "ccXLN", 1)
        EIN_ACT = 0
        for i in range(10):
            att = tick("sGD", 16)
            def gp_ein(g, i=i, a=att, pe_need=(cnt["sPE"] if i >= 1 else 0)):
                if pe_need:
                    g.wait_ge(sems["sPE"], pe_need)
                a(g.indirect_dma_start(
                    out=eint, out_offset=None,
                    in_=agxln_o.ap(),
                    in_offset=bass.IndirectOffsetOnAxis(ap=idxu[:, i:i + 1], axis=0)))
            em("gpsimd", gp_ein)
            gd_t = cnt["sGD"]
            wt("tensor", "sGD", gd_t)
            if EIN_ACT:
                wt("tensor", "sACT", EIN_ACT)
            att = tick("sPE")
            def pe_ein(t, a=att):
                last = None
                for dk in range(8):
                    last = t.transpose(pbig_bf[:, dk * 128:dk * 128 + 128],
                                       eint[:, dk * 128:dk * 128 + 128], idbf[:])
                a(last)
            em("tensor", pe_ein)
            pe_t = cnt["sPE"]
            wt("scalar", "sPE", pe_t)
            att = tick("sACT")
            em("scalar", lambda a, i=i, t=att: t(a.copy(
                einfm3[:, :, i * 128:(i + 1) * 128],
                pbig_bf[:, 0:1024].rearrange("p (dk s) -> p dk s", dk=8))))
            EIN_ACT = cnt["sACT"]
        wibf3 = wibf.rearrange("p (dk f) -> p dk f", dk=8)
        hrel32 = hrel.rearrange("p (m c) -> p m c", m=32)
        hrel4v = hrel4.rearrange("p (gm c) -> p gm c", c=128)
        eosb4 = eosb.rearrange("p (ci w) -> p ci w", ci=4)
        EO_LD = 0
        EO_ACT = 0
        MM1_ACT = {}
        woq = 0
        WO_PE = {}
        for g2 in range(2):
            for m in range(32):
                att = tick("sPE")
                def pe_m1(t, g2=g2, m=m, a=att, w_need=GD_WI_LD, e_need=EIN_ACT,
                          prev_act=(cnt["sACT"] if m >= 1 or g2 >= 1 else 0)):
                    t.wait_ge(sems["sLD"], w_need)
                    t.wait_ge(sems["sACT"], e_need)
                    if prev_act:
                        t.wait_ge(sems["sACT"], prev_act)
                    last = None
                    for q0, q1 in [(0, 512), (512, 640)]:
                        for dk in range(8):
                            last = t.matmul(pyA[:, q0:q1],
                                            wibf3[:, dk, m * 128:(m + 1) * 128],
                                            einfm3[:, dk, g2 * 640 + q0:g2 * 640 + q1],
                                            start=(dk == 0), stop=(dk == 7))
                    a(last)
                em("tensor", pe_m1)
                pe_t = cnt["sPE"]
                wt("scalar", "sPE", pe_t)
                att = tick("sACT", 2)
                def act_m1(a, g2=g2, m=m, t=att):
                    a.activation(hrel32[:, m, :], pyA[:, 0:512], AF.Relu)
                    t(a.activation(hrel4v[:, g2 * 32 + m, :], pyA[:, 512:640], AF.Relu))
                em("scalar", act_m1)
            MM1_ACT[g2] = cnt["sACT"]
            HREL4_ACT = cnt["sACT"]
            for m in range(32):
                par = woq % 2
                if woq >= 2:
                    wt("sync", "sPE", WO_PE[woq - 2])
                if woq == 0:
                    wt("sync", "sGD", GD_WOBF)
                wo_t = dma("sync", wobf[par], wo_bf_d.ap()[m * 128:(m + 1) * 128, :])
                att = tick("sPE")
                def pe_m2(t, g2=g2, m=m, par=par, a=att, w=wo_t,
                          h_need=MM1_ACT[g2], prev_evac=(EO_ACT if m == 0 else 0)):
                    t.wait_ge(sems["sLD"], w)
                    t.wait_ge(sems["sACT"], h_need)
                    if prev_evac:
                        t.wait_ge(sems["sACT"], prev_evac)
                    last = None
                    for ci in range(4):
                        dst = (pbig[:, 0:1024], pbig[:, 1024:2048],
                               pyA[:, :], pyB[:, :])[ci]
                        for q in range(2):
                            last = t.matmul(
                                dst[:, q * 512:(q + 1) * 512],
                                hrel32[:, m, ci * 128:(ci + 1) * 128],
                                wobf[par][:, q * 512:(q + 1) * 512],
                                start=(m == 0), stop=(m == 31))
                    a(last)
                em("tensor", pe_m2)
                WO_PE[woq] = cnt["sPE"]
                woq += 1
            pe_t = cnt["sPE"]
            wt("scalar", "sPE", pe_t)
            if EO_LD:
                wt("scalar", "sLD", EO_LD)
            att = tick("sACT", 4)
            def act_eo(a, t=att):
                srcs = (pbig[:, 0:1024], pbig[:, 1024:2048], pyA[:, :], pyB[:, :])
                last = None
                for ci in range(4):
                    last = a.copy(eosb4[:, ci, :], srcs[ci])
                t(last)
            em("scalar", act_eo)
            EO_ACT = cnt["sACT"]
            wt("sync", "sACT", EO_ACT)
            for ci in range(4):
                EO_LD = dma("sync",
                            ageo_i.ap()[g2 * CAP + ci * 128: g2 * CAP + ci * 128 + 128, :],
                            eosb4[:, ci, :])
        for m in range(32):
            par = woq % 2
            wt("sync", "sPE", WO_PE[woq - 2])
            wo_t = dma("sync", wobf[par], wo_bf_d.ap()[m * 128:(m + 1) * 128, :])
            att = tick("sPE")
            def pe_m2c(t, m=m, par=par, a=att, w=wo_t,
                       h_need=MM1_ACT[1], prev_evac=(EO_ACT if m == 0 else 0)):
                t.wait_ge(sems["sLD"], w)
                t.wait_ge(sems["sACT"], h_need)
                if prev_evac:
                    t.wait_ge(sems["sACT"], prev_evac)
                last = None
                for gi2 in range(2):
                    lh = hrel4v[:, gi2 * 32 + m, :]
                    dst = pbig[:, 0:1024] if gi2 == 0 else pbig[:, 1024:2048]
                    for q in range(2):
                        last = t.matmul(dst[:, q * 512:(q + 1) * 512], lh,
                                        wobf[par][:, q * 512:(q + 1) * 512],
                                        start=(m == 0), stop=(m == 31))
                a(last)
            em("tensor", pe_m2c)
            WO_PE[woq] = cnt["sPE"]
            woq += 1
        pe_t = cnt["sPE"]
        wt("scalar", "sPE", pe_t)
        wt("scalar", "sLD", EO_LD)
        att = tick("sACT", 2)
        def act_eoc(a, t=att):
            a.copy(eosb4[:, 0, :], pbig[:, 0:1024])
            t(a.copy(eosb4[:, 1, :], pbig[:, 1024:2048]))
        em("scalar", act_eoc)
        EO_ACT = cnt["sACT"]
        wt("sync", "sACT", EO_ACT)
        for gi2 in range(2):
            EO_LD = dma("sync",
                        ageo_i.ap()[gi2 * CAP + 512: gi2 * CAP + 640, :],
                        eosb4[:, gi2, :])
        wt("gpsimd", "sLD", EO_LD)
        att = tick("ccEO")
        em("gpsimd", lambda g, t=att: t(g.collective_compute(
            "AllGather", ALU.bypass, replica_groups=[list(range(NCORE))],
            ins=[ageo_i.ap().opt()], outs=[ageo_o.ap().opt()])))

        GDc = {}; XLc = {}; V5c = {}; A7c = {}; V6c = {}; STc = {}
        wt("gpsimd", "ccEO", 1)
        for r in range(NA + 1):
            j = r
            if j < NA:
                p2 = j % 2
                if j >= 2:
                    wt("gpsimd", "sVE", V5c[j - 2])
                att = tick("sGD", 16)
                em("gpsimd", lambda g, j=j, p2=p2, a=att: a(g.indirect_dma_start(
                    out=cmb_g0[p2], out_offset=None, in_=ageo_o.ap(),
                    in_offset=bass.IndirectOffsetOnAxis(ap=gi0a[:, j:j + 1], axis=0))))
                att = tick("sGD", 16)
                em("gpsimd", lambda g, j=j, p2=p2, a=att: a(g.indirect_dma_start(
                    out=cmb_g1[p2], out_offset=None, in_=ageo_o.ap(),
                    in_offset=bass.IndirectOffsetOnAxis(ap=gi1a[:, j:j + 1], axis=0))))
                GDc[j] = cnt["sGD"]
                if j >= 2:
                    wt("sync", "sVE", V5c[j - 2])
                XLc[j] = dma("sync", cmb_xl[p2], xln32_d.ap()[j * 128:(j + 1) * 128, :])
                att = tick("sVE")
                def v5(v, j=j, p2=p2, t=att, gd=GDc[j], ld=XLc[j]):
                    v.wait_ge(sems["sGD"], gd)
                    v.wait_ge(sems["sLD"], ld)
                    v.tensor_scalar(cmb_t1, cmb_g1[p2], g1a[:, j:j + 1], None,
                                    op0=ALU.mult)
                    v.drain()
                    v.scalar_tensor_tensor(cmb_yf[p2], cmb_g0[p2], g0a[:, j:j + 1],
                                           cmb_t1, op0=ALU.mult, op1=ALU.add)
                    v.drain()
                    v.tensor_tensor(cmb_yf[p2], cmb_yf[p2], cmb_xl[p2], op=ALU.add)
                    v.drain()
                    v.bn_stats(cmb_bns[:, 0:6], cmb_yf[p2][:, 0:512])
                    v.bn_stats(cmb_bns[:, 6:12], cmb_yf[p2][:, 512:1024])
                    v.drain()
                    v.bn_aggr(cmb_bna[p2], cmb_bns[:, 0:12])
                    v.drain()
                    v.tensor_scalar(cmb_ms[p2][:, 2:3], cmb_bna[p2][:, 1:2], EPS,
                                    None, op0=ALU.add)
                    t(v.drain())
                em("vector", v5)
                V5c[j] = cnt["sVE"]
            j6 = r - 1
            if 0 <= j6 < NA:
                p2 = j6 % 2
                att = tick("sACT")
                em("scalar", lambda a, p2=p2, t=att, ve=V5c[j6]: (
                    a.wait_ge(sems["sVE"], ve),
                    t(a.activation(cmb_ms[p2][:, 0:1], cmb_ms[p2][:, 2:3], AF.Sqrt)))[-1])
                A7c[j6] = cnt["sACT"]
                att = tick("sVE")
                def v6(v, j=j6, p2=p2, t=att, ac=A7c[j6], stf=STc.get(j6 - 2, 0)):
                    v.wait_ge(sems["sACT"], ac)
                    if stf:
                        v.wait_ge(sems["sST"], stf)
                    v.reciprocal(cmb_ms[p2][:, 1:2], cmb_ms[p2][:, 0:1])
                    v.drain()
                    t(v.tensor_scalar(cmb_o[p2], cmb_yf[p2], cmb_bna[p2][:, 0:1],
                                      cmb_ms[p2][:, 1:2],
                                      op0=ALU.subtract, op1=ALU.mult))
                em("vector", v6)
                V6c[j6] = cnt["sVE"]
                wt("sync", "sVE", V6c[j6])
                att = tick("sST", 16)
                em("sync", lambda e, j=j6, p2=p2, a=att: a(e.dma_start(
                    out=out_t[j * 128:(j + 1) * 128, :], in_=cmb_o[p2])))
                STc[j6] = cnt["sST"]
        wt("sync", "sST", cnt["sST"])

        @block.sync
        def _(e):
            for fn in prog["sync"]:
                fn(e)

        @block.gpsimd
        def _(e):
            for fn in prog["gpsimd"]:
                fn(e)

        @block.vector
        def _(e):
            for fn in prog["vector"]:
                fn(e)

        @block.scalar
        def _(e):
            for fn in prog["scalar"]:
                fn(e)

        @block.tensor
        def _(e):
            for fn in prog["tensor"]:
                fn(e)

    return nc



def _prep_core_inputs(c, xs, W_dt, b_dt, W_B, W_C, A_log, Dv, w_r, wi, wo):
    f32 = np.float32
    d = {}
    d["x"] = np.ascontiguousarray(xs[c * TLOC:(c + 1) * TLOC]).astype(f32, copy=False)
    d["wdt"] = W_dt
    bdt_t = np.zeros((128, 8), f32)
    for ee in range(8):
        bdt_t[:, ee] = b_dt[ee * 128:(ee + 1) * 128]
    d["bdt"] = bdt_t
    d["wbc"] = np.concatenate([W_B, W_C], axis=0).astype(f32, copy=False)
    adiag = np.zeros((128, 2048), f32)
    sel01 = np.zeros((128, 2048), f32)
    dl = np.arange(8)
    n = np.arange(NST)
    for k in range(16):
        rows = (8 * k + dl)[:, None]
        cols = (k * 128 + dl[:, None] * 16 + n[None, :])
        adiag[rows, cols] = A_log[128 * c + 8 * k + dl, :]
        sel01[rows, cols] = 1.0
    d["adiag"] = adiag
    d["sel01"] = sel01
    selbc = np.zeros((32, 256), f32)
    p = np.arange(128)
    selbc[p % 16, p] = 1.0
    selbc[16 + (p % 16), 128 + p] = 1.0
    d["selbc"] = selbc
    selr = np.zeros((128, 8), f32)
    selr[p, p // 16] = 1.0
    d["selr"] = selr
    d["id32"] = np.eye(128, dtype=f32)
    d["idbf"] = np.eye(128, dtype=ml_dtypes.bfloat16)
    d["drep"] = np.broadcast_to(Dv, (128, D)).copy()
    d["wr"] = w_r
    d["wi"] = wi[c]
    d["wo"] = wo[c]
    offsel = np.zeros((128, 16), f32)
    for c1 in range(8):
        if c1 // 4 != c // 4:
            continue
        for k1 in range(2):
            for k in range(2):
                if (k1 < k) or (k1 == k and c1 < c):
                    for e in range(8):
                        offsel[c1 * 16 + k1 * 8 + e, k * 8 + e] = 1.0
    d["offsel"] = offsel
    tok = np.zeros((128, 16), f32)
    for j in range(16):
        tok[:, j] = c * TLOC + j * 128 + np.arange(128)
    d["tokid"] = tok
    d["gofs"] = np.full((128, 1), (c // 4) * CAP, f32)
    mi = np.zeros((128, 10), np.uint32)
    for i in range(10):
        mi[:, i] = c * SLOT + i * 128 + np.arange(128)
    d["mapidx"] = mi
    d["iotaE"] = np.broadcast_to(np.arange(8, dtype=f32), (128, 8)).copy()
    return d


def kernel(L0_attn_x, L0_attn_W_dt, L0_attn_b_dt, L0_attn_W_B, L0_attn_W_C,
           L0_attn_A_log, L0_attn_D, L0_ffn_w_router, L0_ffn_wi, L0_ffn_wo):
    xs = np.ascontiguousarray(np.asarray(L0_attn_x, np.float32).reshape(TTOT, D))
    W_dt = np.ascontiguousarray(np.asarray(L0_attn_W_dt, np.float32))
    b_dt = np.asarray(L0_attn_b_dt, np.float32)
    W_B = np.asarray(L0_attn_W_B, np.float32)
    W_C = np.asarray(L0_attn_W_C, np.float32)
    A_log = np.asarray(L0_attn_A_log, np.float32)
    Dv = np.asarray(L0_attn_D, np.float32)
    w_r = np.ascontiguousarray(np.asarray(L0_ffn_w_router, np.float32))
    wi = np.ascontiguousarray(np.asarray(L0_ffn_wi, np.float32))
    wo = np.ascontiguousarray(np.asarray(L0_ffn_wo, np.float32))

    nc = build_program()
    in_maps = [_prep_core_inputs(c, xs, W_dt, b_dt, W_B, W_C, A_log, Dv, w_r, wi, wo)
               for c in range(NCORE)]
    res = run_bass_kernel_spmd(nc, in_maps, core_ids=list(range(NCORE)))
    out = np.concatenate([res.results[c]["out"] for c in range(NCORE)], axis=0)
    return out.reshape(2, TTOT // 2, D)


# version 29 (final)
# speedup vs baseline: 1.0920x; 1.0920x over previous
import contextlib
import numpy as np
import ml_dtypes
import concourse.bass as bass
import concourse.mybir as mybir
from concourse.bass_utils import run_bass_kernel_spmd
from concourse.aot_env import allow_ici_all_to_all

F32 = mybir.dt.float32
F32R = mybir.dt.float32r
BF16 = mybir.dt.bfloat16
U32 = mybir.dt.uint32
AF = mybir.ActivationFunctionType
ALU = mybir.AluOpType
AX = mybir.AxisListType

NCORE = 8
D = 1024
NST = 16
TLOC = 2048
TTOT = NCORE * TLOC
NA = 16
TB = 2048
NB = 8
NTIL = 16
E = 8
CAP = 640
SLOT = 2 * CAP
MAPN = 10368
MAPW = 8
FF = 4096
EPS = 1e-5

DEBUG = False


def build_program():
    nc = bass.Bass("TRN2", target_bir_lowering=False)

    x_in = nc.dram_tensor("x", [TLOC, D], F32, kind="ExternalInput").ap()
    wdt_in = nc.dram_tensor("wdt", [D, D], F32, kind="ExternalInput").ap()
    bdt_in = nc.dram_tensor("bdt", [128, 8], F32, kind="ExternalInput").ap()
    wbc_in = nc.dram_tensor("wbc", [32, D], F32, kind="ExternalInput").ap()
    adiag_in = nc.dram_tensor("adiag", [128, 2048], F32, kind="ExternalInput").ap()
    sel01_in = nc.dram_tensor("sel01", [128, 2048], F32, kind="ExternalInput").ap()
    selbc_in = nc.dram_tensor("selbc", [32, 256], F32, kind="ExternalInput").ap()
    selr_in = nc.dram_tensor("selr", [128, 8], F32, kind="ExternalInput").ap()
    id32_in = nc.dram_tensor("id32", [128, 128], F32, kind="ExternalInput").ap()
    idbf_in = nc.dram_tensor("idbf", [128, 128], BF16, kind="ExternalInput").ap()
    drep_in = nc.dram_tensor("drep", [128, D], F32, kind="ExternalInput").ap()
    wr_in = nc.dram_tensor("wr", [D, E], F32, kind="ExternalInput").ap()
    wi_in = nc.dram_tensor("wi", [D, FF], F32, kind="ExternalInput").ap()
    wo_in = nc.dram_tensor("wo", [FF, D], F32, kind="ExternalInput").ap()
    offsel_in = nc.dram_tensor("offsel", [128, 16], F32, kind="ExternalInput").ap()
    tokid_in = nc.dram_tensor("tokid", [128, 16], F32, kind="ExternalInput").ap()
    gofs_in = nc.dram_tensor("gofs", [128, 1], F32, kind="ExternalInput").ap()
    mapidx_in = nc.dram_tensor("mapidx", [128, 10], U32, kind="ExternalInput").ap()
    iotaE_in = nc.dram_tensor("iotaE", [128, 8], F32, kind="ExternalInput").ap()
    out_t = nc.dram_tensor("out", [TLOC, D], F32, kind="ExternalOutput").ap()
    dbg = {}
    if DEBUG:
        dbg["dt"] = nc.dram_tensor("dbg_dt", [D, TLOC], F32, kind="ExternalOutput").ap()
        dbg["dtx"] = nc.dram_tensor("dbg_dtx", [D, TLOC], F32, kind="ExternalOutput").ap()
        dbg["y"] = nc.dram_tensor("dbg_y", [D, TLOC], F32, kind="ExternalOutput").ap()
        dbg["xln"] = nc.dram_tensor("dbg_xln", [TLOC, D], F32, kind="ExternalOutput").ap()
        dbg["map"] = nc.dram_tensor("dbg_map", [MAPN, MAPW], F32, kind="ExternalOutput").ap()
        dbg["rt"] = nc.dram_tensor("dbg_rt", [128, 6 * 16], F32, kind="ExternalOutput").ap()
        dbg["bpc"] = nc.dram_tensor("dbg_bpc", [NCORE * 32, TLOC], F32, kind="ExternalOutput").ap()
        dbg["hc"] = nc.dram_tensor("dbg_hc", [128, 16], F32, kind="ExternalOutput").ap()
        dbg["ys"] = nc.dram_tensor("dbg_ys", [D, TLOC], F32, kind="ExternalOutput").ap()
        for nm in ["sa", "s01", "bb", "cb", "dA", "dBx", "h"]:
            dbg[nm] = nc.dram_tensor("dbg_" + nm, [128, 2048], F32, kind="ExternalOutput").ap()
        dbg["ytm"] = nc.dram_tensor("dbg_ytm", [128, 1024], F32, kind="ExternalOutput").ap()
        dbg["r1"] = nc.dram_tensor("dbg_r1", [128, 1024], F32, kind="ExternalOutput").ap()
        dbg["bna"] = nc.dram_tensor("dbg_bna", [128, 2], F32, kind="ExternalOutput").ap()
        dbg["yfl"] = nc.dram_tensor("dbg_yfl", [128, 1024], F32, kind="ExternalOutput").ap()
        dbg["ms"] = nc.dram_tensor("dbg_ms", [128, 4], F32, kind="ExternalOutput").ap()
        dbg["xl0"] = nc.dram_tensor("dbg_xl0", [128, 1024], F32, kind="ExternalOutput").ap()
        dbg["mfm"] = nc.dram_tensor("dbg_mfm", [16, TLOC], F32, kind="ExternalOutput").ap()
        dbg["lcf"] = nc.dram_tensor("dbg_lcf", [16, TLOC], F32, kind="ExternalOutput").ap()
        dbg["o16"] = nc.dram_tensor("dbg_o16", [16, 1], F32, kind="ExternalOutput").ap()
        dbg["pt1"] = nc.dram_tensor("dbg_pt1", [128, 16], F32, kind="ExternalOutput").ap()
        dbg["pc1"] = nc.dram_tensor("dbg_pc1", [128, 24], F32, kind="ExternalOutput").ap()
        dbg["mk1"] = nc.dram_tensor("dbg_mk1", [128, 16], F32, kind="ExternalOutput").ap()

    xfm32_d = nc.dram_tensor("xfm32_d", [D, TLOC], F32)
    a2adt_i = nc.dram_tensor("a2adt_i", [D, TLOC], F32)
    a2adt_o = nc.dram_tensor("a2adt_o", [D, TLOC], F32)
    a2adx_i = nc.dram_tensor("a2adx_i", [D, TLOC], F32)
    a2adx_o = nc.dram_tensor("a2adx_o", [D, TLOC], F32)
    agbpc_i = nc.dram_tensor("agbpc_i", [32, TLOC], F32)
    agbpc_o = nc.dram_tensor("agbpc_o", [NCORE * 32, TLOC], F32, addr_space="Shared")
    a2ay_i = nc.dram_tensor("a2ay_i", [D, TLOC], F32)
    a2ay_o = nc.dram_tensor("a2ay_o", [D, TLOC], F32)
    agxln_i = nc.dram_tensor("agxln_i", [TLOC, D], BF16)
    agxln_o = nc.dram_tensor("agxln_o", [TTOT, D], BF16, addr_space="Shared")
    agcnt_i = nc.dram_tensor("agcnt_i", [16, 8], F32)
    agcnt_o = nc.dram_tensor("agcnt_o", [128, 8], F32)
    map_i = nc.dram_tensor("map_i", [MAPN, MAPW], F32)
    map_o = nc.dram_tensor("map_o", [MAPN, MAPW], F32)
    ageo_i = nc.dram_tensor("ageo_i", [SLOT, D], BF16)
    ageo_o = nc.dram_tensor("ageo_o", [E * SLOT, D], BF16, addr_space="Shared")
    xln32_d = nc.dram_tensor("xln32_d", [TLOC, D], F32)
    wi_bf_d = nc.dram_tensor("wi_bf_d", [D, FF], BF16)
    wo_bf_d = nc.dram_tensor("wo_bf_d", [FF, D], BF16)

    with contextlib.ExitStack() as stk:
        En = stk.enter_context
        En(allow_ici_all_to_all())

        selAbs = En(nc.sbuf_tensor("selAbs", [128, 2048], F32R))
        sel01 = En(nc.sbuf_tensor("sel01r", [128, 2048], F32R))
        selbcr = En(nc.sbuf_tensor("selbcr", [32, 256], F32R))
        selrb = En(nc.sbuf_tensor("selrb", [128, 8], BF16))
        id32 = En(nc.sbuf_tensor("id32s", [128, 128], F32))
        idbf = En(nc.sbuf_tensor("idbfs", [128, 128], BF16))
        d1rep = En(nc.sbuf_tensor("d1rep", [128, D], F32))
        wrsb = En(nc.sbuf_tensor("wrsb", [128, 64], F32))
        bdt = En(nc.sbuf_tensor("bdt_s", [128, 8], F32))
        iotaE = En(nc.sbuf_tensor("iotaE_s", [128, 8], F32))
        tokid = En(nc.sbuf_tensor("tokid_s", [128, 16], F32))
        gofs = En(nc.sbuf_tensor("gofs_s", [128, 1], F32))
        offsel = En(nc.sbuf_tensor("offsel_s", [128, 16], F32))
        mapidx = En(nc.sbuf_tensor("mapidx_s", [128, 10], U32))
        hcar = En(nc.sbuf_tensor("hcar", [128, 16], F32))
        wbcT = En(nc.sbuf_tensor("wbcT", [128, 8 * 32], BF16))
        g0a = En(nc.sbuf_tensor("g0a", [128, 16], F32))
        g1a = En(nc.sbuf_tensor("g1a", [128, 16], F32))
        e0a = En(nc.sbuf_tensor("e0a", [128, 16], F32))
        e1a = En(nc.sbuf_tensor("e1a", [128, 16], F32))
        gi0a = En(nc.sbuf_tensor("gi0a", [128, 16], U32))
        gi1a = En(nc.sbuf_tensor("gi1a", [128, 16], U32))
        cntsb = En(nc.sbuf_tensor("cntsb", [128, 8], F32))
        off16 = En(nc.sbuf_tensor("off16", [16, 1], F32))
        idxu = En(nc.sbuf_tensor("idxu", [128, 10], U32))
        idxf = En(nc.sbuf_tensor("idxf", [128, 80], F32))
        mapz = En(nc.sbuf_tensor("mapz", [128, MAPN * MAPW // 128], F32))
        dtrb = En(nc.sbuf_tensor("dtrb_s", [128, 2048], F32R))
        dtxrb = En(nc.sbuf_tensor("dtxrb_s", [128, 2048], F32R))
        bpcrb = En(nc.sbuf_tensor("bpcrb_s", [32, 2048], F32R))

        ARN = En(nc.sbuf_tensor("ARN", [128, 36864], F32))

        def arn(off, n, dt=F32):
            v = ARN[:, off:off + n]
            if dt != F32:
                v = v.bitcast(dt)
            return v

        xfmbf = arn(0, 8192, BF16)
        xfmbf3 = xfmbf.rearrange("p (dk t) -> p dk t", dk=8)
        wdt32 = arn(8192, 8192)
        xstg = arn(8192, 4096)
        xbfst = arn(12288, 512, BF16)
        st32 = arn(12800, 1024)
        adexp = arn(20480, 2048)
        wdtbf = arn(16384, 4096, BF16)
        dtloc = arn(16384, 2048)
        taux1 = arn(18432, 2048)
        taux2 = arn(20480, 2048)
        xf32ec = arn(22528, 2048)
        bpcsb = arn(24576, 2048)
        wbc32 = arn(26624, 1024)
        wbcbf = arn(27648, 512, BF16)
        wdtT = arn(28672, 4096, BF16)

        dtb = arn(0, 2048)
        dtxb = arn(2048, 2048)
        bpcb = arn(4096, 2048)
        bpbc = arn(6144, 2048)
        cbc = arn(8192, 1024, BF16)
        dAb = [arn(9216, 2048), arn(11264, 2048)]
        dBxb = [arn(13312, 2048), arn(15360, 2048)]
        hhb = [arn(17408, 2048), arn(19456, 2048)]
        hCb = [arn(21504 + 1024 * i, 1024, BF16) for i in range(6)]
        ystg = [arn(27648, 2048), arn(29696, 2048)]

        f_yfl = [arn(0, 1024), arn(1024, 1024)]
        f_ytm = [arn(2048, 1024), arn(3072, 1024)]
        f_x = [arn(4096, 1024), arn(5120, 1024)]
        f_r1 = arn(6144, 1024)
        f_xln = [arn(7168, 1024), arn(8192, 1024)]
        f_xlf = arn(9216, 1024)
        f_xlb = [arn(10240, 512, BF16), arn(10752, 512, BF16)]
        f_bns = arn(11264, 16)
        f_bna = arn(11280, 2)
        f_ms = arn(11282, 4)
        f_lg = arn(11286, 8)
        f_prb = arn(11294, 8)
        f_red = arn(11302, 4)
        f_msk = [arn(11306, 16), arn(11322, 16)]
        f_tp = arn(11338, 8)
        f_tpi = arn(11346, 8, U32)

        lcfm = arn(21504, 2048)
        zer16 = arn(23552, 2048)
        maskfm = arn(25600, 2048)
        postm = arn(31744, 16)
        p_t8 = arn(31760, 8)
        p_c1 = arn(31768, 24)
        p_u = arn(31792, 4, U32)
        tokrep = arn(31796, 8)

        wibf = arn(0, 16384, BF16)
        einfm = arn(16384, 5120, BF16)
        einfm3 = einfm.rearrange("p (dk s) -> p dk s", dk=8)
        hrel = arn(21504, 8192, BF16)
        hrel4 = arn(29696, 4096, BF16)
        wobf = [arn(35840, 512, BF16), arn(36352, 512, BF16)]
        eosb = arn(33792, 2048, BF16)
        eint = arn(33792, 512, BF16)

        cmb_t1 = arn(0, 1024)
        cmb_yf = [arn(1024, 1024), arn(2048, 1024)]
        cmb_xl = [arn(3072, 1024), arn(4096, 1024)]
        cmb_o = [arn(5120, 1024), arn(6144, 1024)]
        cmb_bns = arn(7168, 16)
        cmb_bna = [arn(7184, 2), arn(7186, 2)]
        cmb_ms = [arn(7188, 6), arn(7194, 6)]
        cmb_g0 = [arn(7200, 512, BF16), arn(7712, 512, BF16)]
        cmb_g1 = [arn(8224, 512, BF16), arn(8736, 512, BF16)]

        pbig = En(nc.psum_tensor("pbig", [128, 2048], F32))
        pyA = En(nc.psum_tensor("pyA", [128, 1024], F32))
        pyB = En(nc.psum_tensor("pyB", [128, 1024], F32))
        pbig_bf = pbig.bitcast(BF16)

        semnames = ["sLD", "sGD", "sPE", "sACT", "sVE", "sPO", "sST",
                    "qPE1", "qACT1", "qVE1", "qVE2", "qPO", "qPE2", "qACT2",
                    "ccDT", "ccDX", "ccBPC", "ccY", "ccXLN", "ccCNT", "ccMAP", "ccEO"]
        sems = {nm: En(nc.semaphore(nm)) for nm in semnames}
        cnt = {nm: 0 for nm in semnames}

        block = En(nc.Block())
        prog = {k: [] for k in ["sync", "gpsimd", "vector", "scalar", "tensor"]}

        def em(q, fn):
            prog[q].append(fn)

        def tick(nm, k=1):
            cnt[nm] += k
            s = sems[nm]
            def att(instr):
                instr.then_inc(s, k)
            return att

        def wt(q, nm, val=None):
            v = cnt[nm] if val is None else val
            if v <= 0:
                return
            s = sems[nm]
            em(q, lambda e, s=s, v=v: e.wait_ge(s, v))

        def dma(q, dst, src, semnm=None, slow=False):
            semnm = semnm or ("sGD" if q == "gpsimd" else "sLD")
            att = tick(semnm, 16)
            em(q, lambda e, d=dst, s=src, a=att, sl=slow: a(
                e.dma_start(out=d, in_=s, allow_slow_non_contiguous=sl)))
            return cnt[semnm]

        dma("sync", id32[:], id32_in)
        dma("sync", idbf[:], idbf_in)
        dma("sync", wrsb[:].rearrange("p (c e) -> p c e", c=8),
            wr_in.rearrange("(c p) e -> p c e", p=128))
        dma("sync", bdt[:], bdt_in)
        dma("sync", tokid[:], tokid_in)
        dma("sync", gofs[:], gofs_in)
        dma("sync", offsel[:], offsel_in)
        dma("sync", mapidx[:], mapidx_in)
        dma("sync", iotaE[:], iotaE_in)
        dma("sync", adexp, adiag_in)
        dma("sync", d1rep[:], drep_in)
        dma("sync", wdt32.rearrange("p (ee d) -> p ee d", ee=8),
            wdt_in.rearrange("(ee p) d -> p ee d", p=128))
        P0_LD = dma("sync", wbc32[0:32, :], wbc_in)
        dma("gpsimd", selrb[:], selr_in)
        dma("gpsimd", sel01[:], sel01_in)
        P0_GD = dma("gpsimd", selbcr[:], selbc_in)

        wt("vector", "sLD", P0_LD)
        em("vector", lambda v: v.tensor_scalar(d1rep[:], d1rep[:], 1.0, None, op0=ALU.add))
        em("vector", lambda v: v.memset(hcar[:], 0.0))
        att = tick("sVE")
        em("vector", lambda v, a=att: a(v.memset(mapz[:], 0.0)))
        VE_INIT = cnt["sVE"]

        wt("scalar", "sLD", P0_LD)
        att = tick("sACT")
        em("scalar", lambda a, t=att: t(a.activation(xf32ec, adexp, AF.Exp)))
        wt("vector", "sACT", cnt["sACT"])
        wt("vector", "sGD", P0_GD)
        att = tick("sVE")
        em("vector", lambda v, t=att: t(v.tensor_tensor(
            xf32ec, xf32ec, sel01[:].bitcast(F32), op=ALU.mult)))
        wt("gpsimd", "sVE", cnt["sVE"])
        GD_SELABS = dma("gpsimd", selAbs[:], xf32ec)

        GD_WIBF = dma("gpsimd", wi_bf_d.ap(), wi_in)
        GD_WOBF = dma("gpsimd", wo_bf_d.ap(), wo_in)

        att = tick("sACT")
        em("scalar", lambda a, t=att: t(a.copy(wdtbf, wdt32)))
        att = tick("sACT")
        em("scalar", lambda a, t=att: t(a.copy(wbcbf[0:32, :], wbc32[0:32, :])))
        ACT_WCAST = cnt["sACT"]

        wt("tensor", "sACT", ACT_WCAST)
        wt("tensor", "sLD", P0_LD)
        prev_evac = 0
        for bi in range(8):
            half = bi % 2
            att = tick("sPE")
            def pe_tr(t, bi=bi, half=half, need=(prev_evac if bi >= 2 else 0), a=att):
                if need:
                    t.wait_ge(sems["sACT"], need)
                last = None
                for dk in range(8):
                    last = t.transpose(
                        pbig_bf[:, half * 2048 + dk * 128: half * 2048 + dk * 128 + 128],
                        wdtbf[:, bi * 1024 + dk * 128: bi * 1024 + dk * 128 + 128],
                        idbf[:])
                a(last)
            em("tensor", pe_tr)
            pe_t = cnt["sPE"]
            att = tick("sACT")
            def act_ev(a, bi=bi, half=half, need=pe_t, t=att):
                a.wait_ge(sems["sPE"], need)
                t(a.copy(wdtT[:, bi * 1024:(bi + 1) * 1024],
                         pbig_bf[:, half * 2048: half * 2048 + 1024]))
            em("scalar", act_ev)
            prev_evac = cnt["sACT"]
        att = tick("sPE")
        def pe_wbc(t, need=prev_evac, a=att):
            t.wait_ge(sems["sACT"], need)
            last = None
            for dk in range(8):
                last = t.transpose(pbig_bf[:, dk * 32: dk * 32 + 32],
                                   wbcbf[0:32, dk * 128:dk * 128 + 128],
                                   idbf[0:32, 0:32])
            a(last)
        em("tensor", pe_wbc)
        pe_t = cnt["sPE"]
        att = tick("sACT")
        def act_wbc(a, need=pe_t, t=att):
            a.wait_ge(sems["sPE"], need)
            t(a.copy(wbcT[:], pbig_bf[:, 0:256]))
        em("scalar", act_wbc)
        W_ACT = cnt["sACT"]
        W_PE = cnt["sPE"]

        PE_A = {}
        ACT_A = {}
        ST_A = {}
        xfm32v = xfm32_d.ap().rearrange("(dk q) t -> q dk t", q=128)
        for a_i in range(NA):
            sl = a_i % 4
            xa = xstg[:, sl * 1024:(sl + 1) * 1024]
            if a_i >= 4:
                wt("sync", "sPE", PE_A[a_i - 4])
            else:
                wt("sync", "sACT", ACT_WCAST)
            ld_t = dma("sync", xa, x_in[a_i * 128:(a_i + 1) * 128, :])
            wt("scalar", "sLD", ld_t)
            if a_i >= 1:
                wt("scalar", "sPE", PE_A[a_i - 1])
            att = tick("sACT")
            em("scalar", lambda a, s=xa, t=att: t(a.copy(xbfst, s)))
            actc = cnt["sACT"]
            att = tick("sPE")
            def pe_a(t, actc=actc, xa=xa, a=att,
                     prev=(ACT_A[a_i - 1] if a_i >= 1 else W_ACT)):
                t.wait_ge(sems["sACT"], actc)
                t.wait_ge(sems["sACT"], prev)
                for dk in range(8):
                    t.transpose(pbig_bf[:, dk * 128: dk * 128 + 128],
                                xbfst[:, dk * 128:dk * 128 + 128], idbf[:])
                last = None
                for dk in range(8):
                    last = t.transpose(pbig[:, 512 + dk * 128: 512 + dk * 128 + 128],
                                       xa[:, dk * 128:dk * 128 + 128], id32[:])
                a(last)
            em("tensor", pe_a)
            PE_A[a_i] = cnt["sPE"]
            wt("scalar", "sPE", PE_A[a_i])
            if a_i >= 1:
                wt("scalar", "sLD", ST_A[a_i - 1])
            att = tick("sACT", 2)
            def act_a(a, a_i=a_i, t=att):
                a.copy(xfmbf3[:, :, a_i * 128:(a_i + 1) * 128],
                       pbig_bf[:, 0:1024].rearrange("p (dk t) -> p dk t", dk=8))
                t(a.copy(st32, pbig[:, 512:1536]))
            em("scalar", act_a)
            ACT_A[a_i] = cnt["sACT"]
            wt("sync", "sACT", ACT_A[a_i])
            ST_A[a_i] = dma("sync", xfm32v[:, :, a_i * 128:(a_i + 1) * 128],
                            st32.rearrange("q (dk t) -> q dk t", dk=8))
        A_LD = cnt["sLD"]

        att = tick("sPE")
        def pe_bpc(t, a=att, need=ACT_A[NA - 1]):
            t.wait_ge(sems["sACT"], need)
            last = None
            for q in range(4):
                for dk in range(8):
                    last = t.matmul(pbig[0:32, q * 512:(q + 1) * 512],
                                    wbcT[:, dk * 32:dk * 32 + 32],
                                    xfmbf3[:, dk, q * 512:(q + 1) * 512],
                                    start=(dk == 0), stop=(dk == 7))
            a(last)
        em("tensor", pe_bpc)
        pe_t = cnt["sPE"]
        wt("scalar", "sPE", pe_t)
        att = tick("sACT")
        em("scalar", lambda a, t=att: t(a.copy(bpcsb[0:32, :], pbig[0:32, :])))
        ACT_BPC = cnt["sACT"]
        wt("sync", "sACT", ACT_BPC)
        BPC_LD = dma("sync", agbpc_i.ap(), bpcsb[0:32, :])
        wt("gpsimd", "sLD", BPC_LD)
        att = tick("ccBPC")
        em("gpsimd", lambda g, t=att: t(g.collective_compute(
            "AllGather", ALU.bypass, replica_groups=[list(range(NCORE))],
            ins=[agbpc_i.ap().opt()], outs=[agbpc_o.ap().opt()])))

        VE_B = {}
        ST_B = {}
        for ec in range(8):
            att = tick("sPE")
            def pe_dt(t, ec=ec, a=att, na=ACT_A[NA - 1],
                      nv=(VE_B[ec - 1] if ec >= 1 else 0), bp=ACT_BPC,
                      pa=cnt["sACT"]):
                t.wait_ge(sems["sACT"], na)
                t.wait_ge(sems["sACT"], bp)
                t.wait_ge(sems["sACT"], pa)
                if nv:
                    t.wait_ge(sems["sVE"], nv)
                last = None
                for q in range(4):
                    for dk in range(8):
                        last = t.matmul(pbig[:, q * 512:(q + 1) * 512],
                                        wdtT[:, (ec * 8 + dk) * 128:(ec * 8 + dk) * 128 + 128],
                                        xfmbf3[:, dk, q * 512:(q + 1) * 512],
                                        start=(dk == 0), stop=(dk == 7))
                a(last)
            em("tensor", pe_dt)
            pe_t = cnt["sPE"]
            wt("sync", "sLD", A_LD)
            if ec == 0:
                wt("sync", "sGD", GD_SELABS)
            if ec >= 1:
                wt("sync", "sVE", VE_B[ec - 1])
            xf_t = dma("sync", xf32ec, xfm32_d.ap()[ec * 128:(ec + 1) * 128, :])
            wt("scalar", "sPE", pe_t)
            if ec == 0:
                wt("scalar", "sGD", GD_SELABS)
            if ec >= 1:
                wt("scalar", "sLD", ST_B[ec - 1])
            att = tick("sACT")
            def act_sp(a, ec=ec, t=att):
                a.activation(taux1, pbig[:, :], AF.Abs, bias=bdt[:, ec:ec + 1])
                a.activation(taux1, taux1, AF.Exp, scale=-1.0)
                t(a.activation(taux1, taux1, AF.Ln, bias=1.0))
            em("scalar", act_sp)
            act_t = cnt["sACT"]
            wt("vector", "sPE", pe_t)
            wt("vector", "sACT", act_t)
            wt("vector", "sLD", xf_t)
            if ec >= 1:
                wt("vector", "sLD", ST_B[ec - 1])
            att = tick("sVE")
            def ve_dt(v, ec=ec, t=att):
                v.tensor_scalar(taux2, pbig[:, :], bdt[:, ec:ec + 1], 0.0,
                                op0=ALU.add, op1=ALU.max)
                v.tensor_tensor(dtloc, taux1, taux2, op=ALU.add)
                t(v.tensor_tensor(taux2, dtloc, xf32ec, op=ALU.mult))
            em("vector", ve_dt)
            VE_B[ec] = cnt["sVE"]
            wt("sync", "sVE", VE_B[ec])
            dma("sync", a2adt_i.ap()[ec * 128:(ec + 1) * 128, :], dtloc)
            ST_B[ec] = dma("sync", a2adx_i.ap()[ec * 128:(ec + 1) * 128, :], taux2)
        B_LD = cnt["sLD"]

        wt("gpsimd", "sLD", B_LD)
        att = tick("ccDT")
        em("gpsimd", lambda g, t=att: t(g.collective_compute(
            "AllToAll", ALU.bypass, replica_groups=[list(range(NCORE))],
            ins=[a2adt_i.ap().opt()], outs=[a2adt_o.ap().opt()])))
        att = tick("ccDX")
        em("gpsimd", lambda g, t=att: t(g.collective_compute(
            "AllToAll", ALU.bypass, replica_groups=[list(range(NCORE))],
            ins=[a2adx_i.ap().opt()], outs=[a2adx_o.ap().opt()])))

        if DEBUG:
            wt("sync", "ccDT", 1)
            wt("sync", "ccDX", 1)
            dma("sync", dbg["dt"], a2adt_o.ap())
            dma("sync", dbg["dtx"], a2adx_o.ap())

        wt("sync", "ccDT", 1)
        wt("sync", "ccDX", 1)
        wt("sync", "ccBPC", 1)
        wt("sync", "sVE", cnt["sVE"])
        wt("sync", "sGD", cnt["sGD"])

        B_VE_FINAL = cnt["sVE"]
        B_ACT_FINAL = cnt["sACT"]
        pend_pe2 = []
        pend_act2 = []
        LDS = {}
        GDS = {}
        ACT_BC = {}
        YLD = {}
        ACT2_grp = {}
        pe1_u = 0
        for b in range(NB):
            if b >= 1:
                wt("sync", "sGD", GDS[b - 1])
                wt("sync", "qPO", b * NTIL)
                wt("sync", "qVE1", 4 * b * NTIL)
            dma("sync", dtb, a2adt_o.ap()[b * 128:(b + 1) * 128, :])
            dma("sync", dtxb, a2adx_o.ap()[b * 128:(b + 1) * 128, :])
            LDS[b] = dma("sync", bpcb[0:32, :], agbpc_o.ap()[b * 32:(b + 1) * 32, :])
            wt("gpsimd", "sLD", LDS[b])
            if b >= 1:
                wt("gpsimd", "qPE1", 4 * b * NTIL)
            dma("gpsimd", dtrb[:], dtb)
            dma("gpsimd", dtxrb[:], dtxb)
            GDS[b] = dma("gpsimd", bpcrb[:], bpcb[0:32, :])
            att = tick("sPE")
            def pe_bc1(t, b=b, a=att, gd=GDS[b]):
                t.wait_ge(sems["sGD"], gd)
                if b >= 1:
                    t.wait_ge(sems["qACT1"], 4 * b * NTIL)
                    t.wait_ge(sems["qVE1"], 4 * b * NTIL)
                else:
                    t.wait_ge(sems["sVE"], B_VE_FINAL)
                    t.wait_ge(sems["sACT"], B_ACT_FINAL)
                last = None
                for ch in range(4):
                    last = t.matmul(pbig[:, ch * 512:(ch + 1) * 512],
                                    selbcr[:, 0:128],
                                    bpcrb[0:32, ch * 512:(ch + 1) * 512],
                                    start=True, stop=True)
                a(last)
            em("tensor", pe_bc1)
            pe_t = cnt["sPE"]
            wt("scalar", "sPE", pe_t)
            att = tick("sACT")
            em("scalar", lambda a, t=att: t(a.copy(bpbc, pbig[:, :])))
            act_t = cnt["sACT"]
            att = tick("sPE")
            def pe_bc2(t, a=att, need=act_t):
                t.wait_ge(sems["sACT"], need)
                last = None
                for ch in range(4):
                    last = t.matmul(pbig[:, ch * 512:(ch + 1) * 512],
                                    selbcr[:, 128:256],
                                    bpcrb[0:32, ch * 512:(ch + 1) * 512],
                                    start=True, stop=True)
                a(last)
            em("tensor", pe_bc2)
            pe_t = cnt["sPE"]
            wt("scalar", "sPE", pe_t)
            att = tick("sACT")
            em("scalar", lambda a, t=att: t(a.copy(cbc, pbig[:, :])))
            ACT_BC[b] = cnt["sACT"]

            for k in range(NTIL):
                g = b * NTIL + k
                par = g % 2
                grp = k // 4
                jj = k % 4
                gid = b * 4 + grp
                while pend_pe2 and len(pend_pe2) >= 4 and pend_pe2[3][0] is not None \
                        and pend_pe2[3][0] <= g - 2:
                    for _ in range(4):
                        em("tensor", pend_pe2.pop(0)[1])
                while pend_act2 and pend_act2[0][0] <= g - 2:
                    em("scalar", pend_act2.pop(0)[1])
                for q in range(4):
                    u = 4 * g + q
                    sl = u % 2
                    att = tick("qPE1")
                    def pe1(t, k=k, q=q, u=u, sl=sl, a=att, bc=ACT_BC[b]):
                        t.wait_ge(sems["sACT"], bc)
                        if u >= 2:
                            t.wait_ge(sems["qACT1"], u - 1)
                            t.wait_ge(sems["qVE1"], u - 1)
                        c0 = q * 512
                        t.matmul(pbig[:, sl * 512:(sl + 1) * 512],
                                 selAbs[:, k * 128:(k + 1) * 128],
                                 dtrb[:, c0:c0 + 512], start=True, stop=True)
                        a(t.matmul(pbig[:, 1024 + sl * 512:1024 + (sl + 1) * 512],
                                   sel01[:, k * 128:(k + 1) * 128],
                                   dtxrb[:, c0:c0 + 512], start=True, stop=True))
                    em("tensor", pe1)
                    att = tick("qACT1")
                    def act1(a, par=par, q=q, u=u, sl=sl, g=g, t=att):
                        a.wait_ge(sems["qPE1"], u + 1)
                        if g >= 2 and q == 0:
                            a.wait_ge(sems["qVE2"], g - 1)
                        t(a.activation(dAb[par][:, q * 512:(q + 1) * 512],
                                       pbig[:, sl * 512:(sl + 1) * 512],
                                       AF.Exp, scale=-1.0))
                    em("scalar", act1)
                    att = tick("qVE1")
                    def ve1(v, par=par, q=q, u=u, sl=sl, t=att):
                        v.wait_ge(sems["qPE1"], u + 1)
                        t(v.tensor_tensor(dBxb[par][:, q * 512:(q + 1) * 512],
                                          pbig[:, 1024 + sl * 512:1024 + (sl + 1) * 512],
                                          bpbc[:, q * 512:(q + 1) * 512],
                                          op=ALU.mult))
                    em("vector", ve1)
                att = tick("qVE2")
                init0 = (b % 4 == 0)
                prev_copy = None
                if g >= 1:
                    bp, kp = (g - 1) // NTIL, (g - 1) % NTIL
                    if bp % 4 != 3:
                        prev_copy = (kp, (g - 1) % 2)
                def ve2(v, par=par, k=k, g=g, init0=init0, t=att, pc=prev_copy):
                    v.wait_ge(sems["qACT1"], 4 * g + 4)
                    if g >= 2:
                        v.wait_ge(sems["qPO"], g - 1)
                    ini = 0.0 if init0 else hcar[:, k:k + 1]
                    v.tensor_tensor_scan(hhb[par], dAb[par], dBxb[par], ini,
                                         op0=ALU.mult, op1=ALU.add)
                    if pc is not None:
                        t(v.tensor_copy(hcar[:, pc[0]:pc[0] + 1],
                                        hhb[pc[1]][:, 2047:2048]))
                    else:
                        t(v.nop())
                em("vector", ve2)
                if DEBUG and b == 0 and k == 0:
                    wt("sync", "sGD", GD_SELABS)
                    dma("sync", dbg["sa"], selAbs[:].bitcast(F32), semnm="sST")
                    dma("sync", dbg["s01"], sel01[:].bitcast(F32), semnm="sST")
                    wt("sync", "sACT", ACT_BC[0])
                    dma("sync", dbg["bb"], bpbc, semnm="sST")
                    dma("gpsimd", dbg["cb"], cbc, semnm="sST")
                    wt("sync", "qACT1", 4)
                    wt("sync", "qVE1", 4)
                    dma("sync", dbg["dA"], dAb[0], semnm="sST")
                    dma("sync", dbg["dBx"], dBxb[0], semnm="sST")
                    wt("sync", "qVE2", 1)
                    dma("sync", dbg["h"], hhb[0], semnm="sST")
                    DBG_STALL = cnt["sST"]
                att = tick("qPO")
                def po(gp, par=par, g=g, b=b, t=att, bc=ACT_BC[b]):
                    gp.wait_ge(sems["qVE2"], g + 1)
                    gp.wait_ge(sems["sACT"], bc)
                    if g >= 6:
                        gp.wait_ge(sems["qPE2"], g - 5)
                    t(gp.tensor_tensor(hCb[g % 6], hhb[par], cbc, op=ALU.mult))
                em("gpsimd", po)
                att = tick("qPE2")
                def pe2(t, jj=jj, g=g, gid=gid, a=att):
                    t.wait_ge(sems["qPO"], g + 1)
                    if jj == 0 and gid >= 1:
                        t.wait_ge(sems["qACT2"], gid)
                    last = None
                    for q in range(2):
                        t.matmul(pyA[32 * jj:32 * jj + 8, q * 512:(q + 1) * 512],
                                 selrb[:], hCb[g % 6][:, q * 512:(q + 1) * 512],
                                 start=True, stop=True, tile_position=(0, 32 * jj))
                        last = t.matmul(
                            pyB[32 * jj:32 * jj + 8, q * 512:(q + 1) * 512],
                            selrb[:], hCb[g % 6][:, 1024 + q * 512:1024 + (q + 1) * 512],
                            start=True, stop=True, tile_position=(0, 32 * jj))
                    a(last)
                pend_pe2.append((g if jj == 3 else None, pe2))
                if jj == 3:
                    att = tick("qACT2")
                    def act2(a, g=g, gid=gid, t=att,
                             yld=(YLD[gid - 2] if gid >= 2 else 0)):
                        a.wait_ge(sems["qPE2"], g + 1)
                        if yld:
                            a.wait_ge(sems["sLD"], yld)
                        gpar = gid % 2
                        a.copy(ystg[gpar][:, 0:1024], pyA[:, :])
                        t(a.copy(ystg[gpar][:, 1024:2048], pyB[:, :]))
                    pend_act2.append((g, act2))
                    if k == NTIL - 1:
                        for _, fn in pend_pe2:
                            em("tensor", fn)
                        pend_pe2.clear()
                        for _, fn in pend_act2:
                            em("scalar", fn)
                        pend_act2.clear()
                    wt("sync", "qACT2", cnt["qACT2"])
                    gpar = gid % 2
                    for j4 in range(4):
                        krow = grp * 4 + j4
                        dma("sync",
                            a2ay_i.ap()[b * 128 + krow * 8: b * 128 + krow * 8 + 8, :],
                            ystg[gpar][32 * j4:32 * j4 + 8, :])
                    YLD[gid] = cnt["sLD"]

        for _, fn in pend_pe2:
            em("tensor", fn)
        for _, fn in pend_act2:
            em("scalar", fn)
        pend_pe2 = []
        pend_act2 = []
        wt("gpsimd", "sLD", cnt["sLD"])
        att = tick("ccY")
        em("gpsimd", lambda g, t=att: t(g.collective_compute(
            "AllToAll", ALU.bypass, replica_groups=[list(range(NCORE))],
            ins=[a2ay_i.ap().opt()], outs=[a2ay_o.ap().opt()])))

        if DEBUG:
            wt("sync", "ccY", 1)
            dma("sync", dbg["y"], a2ay_o.ap())
            dma("sync", dbg["ys"], a2ay_i.ap())
            dma("sync", dbg["bpc"], agbpc_o.ap())
            wt("sync", "qVE2", NB * NTIL)
            dma("sync", dbg["hc"], hcar[:])

        SCAN_VE = cnt["qVE2"]
        wt("sync", "ccY", 1)
        wt("sync", "qVE2", SCAN_VE)
        wt("sync", "qPO", cnt["qPO"])
        wt("tensor", "ccY", 1)
        yv = a2ay_o.ap().rearrange("(dk q) t -> q dk t", q=128)
        T1c = {}; T2c = {}; T3c = {}; T4c = {}
        A1c = {}; A2c = {}; A3c = {}; A4c = {}; A5c = {}; A6c = {}
        VAc = {}; VBc = {}; VCc = {}; VDc = {}
        S1c = {}; S2c = {}; LDc = {}
        for r in range(NA + 2):
            j = r
            if j < NA:
                p2 = j % 2
                if j >= 2:
                    wt("sync", "sPE", T1c[j - 2])
                    wt("sync", "sVE", VAc[j - 2])
                dma("sync", f_yfl[p2].rearrange("q (dk t) -> q dk t", dk=8),
                    yv[:, :, j * 128:(j + 1) * 128])
                LDc[j] = dma("sync", f_x[p2], x_in[j * 128:(j + 1) * 128, :])
                att = tick("sPE")
                def t1(t, j=j, p2=p2, a=att, ld=LDc[j],
                       af=(A1c.get(j - 1, 0))):
                    t.wait_ge(sems["sLD"], ld)
                    if af:
                        t.wait_ge(sems["sACT"], af)
                    if j == 0:
                        t.wait_ge(sems["qACT1"], 4 * NB * NTIL)
                        t.wait_ge(sems["qVE1"], 4 * NB * NTIL)
                    last = None
                    for dk in range(8):
                        last = t.transpose(pbig[:, dk * 128:dk * 128 + 128],
                                           f_yfl[p2][:, dk * 128:dk * 128 + 128],
                                           id32[:])
                    a(last)
                em("tensor", t1)
                T1c[j] = cnt["sPE"]
                att = tick("sACT")
                def a1(a, j=j, p2=p2, t=att, pe=T1c[j], vf=(VAc.get(j - 2, 0))):
                    a.wait_ge(sems["sPE"], pe)
                    if vf:
                        a.wait_ge(sems["sVE"], vf)
                    t(a.copy(f_ytm[p2], pbig[:, 0:1024]))
                em("scalar", a1)
                A1c[j] = cnt["sACT"]
                att = tick("sVE")
                def v1(v, j=j, p2=p2, t=att, ac=A1c[j], ld=LDc[j]):
                    v.wait_ge(sems["sACT"], ac)
                    v.wait_ge(sems["sLD"], ld)
                    v.tensor_tensor(f_r1, f_x[p2], d1rep[:], op=ALU.mult)
                    v.tensor_tensor(f_r1, f_r1, f_ytm[p2], op=ALU.add)
                    v.drain()
                    v.bn_stats(f_bns[:, 0:6], f_r1[:, 0:512])
                    v.bn_stats(f_bns[:, 6:12], f_r1[:, 512:1024])
                    v.drain()
                    v.bn_aggr(f_bna, f_bns[:, 0:12])
                    v.drain()
                    v.tensor_scalar(f_ms[:, 2:3], f_bna[:, 1:2], EPS, None, op0=ALU.add)
                    t(v.drain())
                em("vector", v1)
                VAc[j] = cnt["sVE"]
                att = tick("sACT")
                em("scalar", lambda a, t=att, ve=VAc[j]: (
                    a.wait_ge(sems["sVE"], ve),
                    t(a.activation(f_ms[:, 0:1], f_ms[:, 2:3], AF.Sqrt)))[-1])
                A2c[j] = cnt["sACT"]
                att = tick("sVE")
                def v2(v, j=j, p2=p2, t=att, ac=A2c[j],
                       t2f=T2c.get(j - 2, 0), s1f=S1c.get(j - 2, 0)):
                    v.wait_ge(sems["sACT"], ac)
                    if t2f:
                        v.wait_ge(sems["sPE"], t2f)
                    if s1f:
                        v.wait_ge(sems["sLD"], s1f)
                    v.reciprocal(f_ms[:, 1:2], f_ms[:, 0:1])
                    v.drain()
                    t(v.tensor_scalar(f_xln[p2], f_r1, f_bna[:, 0:1], f_ms[:, 1:2],
                                      op0=ALU.subtract, op1=ALU.mult))
                em("vector", v2)
                VBc[j] = cnt["sVE"]
                wt("sync", "sVE", VBc[j])
                S1c[j] = dma("sync", xln32_d.ap()[j * 128:(j + 1) * 128, :], f_xln[p2])
                att = tick("sACT")
                def a3(a, j=j, p2=p2, t=att, ve=VBc[j], s2f=S2c.get(j - 2, 0)):
                    a.wait_ge(sems["sVE"], ve)
                    if s2f:
                        a.wait_ge(sems["sLD"], s2f)
                    t(a.copy(f_xlb[p2], f_xln[p2]))
                em("scalar", a3)
                A3c[j] = cnt["sACT"]
                wt("sync", "sACT", A3c[j])
                S2c[j] = dma("sync", agxln_i.ap()[j * 128:(j + 1) * 128, :], f_xlb[p2])
            j2 = r - 1
            if 0 <= j2 < NA:
                p2 = j2 % 2
                att = tick("sPE")
                def t2(t, j=j2, p2=p2, a=att, ve=VBc[j2], af=A4c.get(j2 - 1, 0)):
                    t.wait_ge(sems["sVE"], ve)
                    if af:
                        t.wait_ge(sems["sACT"], af)
                    last = None
                    for dk in range(8):
                        last = t.transpose(
                            pbig[:, 1024 + dk * 128:1024 + dk * 128 + 128],
                            f_xln[p2][:, dk * 128:dk * 128 + 128], id32[:])
                    a(last)
                em("tensor", t2)
                T2c[j2] = cnt["sPE"]
                att = tick("sACT")
                def a4(a, t=att, pe=T2c[j2], t3f=T3c.get(j2 - 1, 0)):
                    a.wait_ge(sems["sPE"], pe)
                    if t3f:
                        a.wait_ge(sems["sPE"], t3f)
                    t(a.copy(f_xlf, pbig[:, 1024:2048]))
                em("scalar", a4)
                A4c[j2] = cnt["sACT"]
                att = tick("sPE")
                def t3(t, a=att, ac=A4c[j2], vcf=VCc.get(j2 - 1, 0)):
                    t.wait_ge(sems["sACT"], ac)
                    if vcf:
                        t.wait_ge(sems["sVE"], vcf)
                    last = None
                    for dk in range(8):
                        last = t.matmul(pyA[:, 0:8], f_xlf[:, dk * 128:dk * 128 + 128],
                                        wrsb[:, dk * 8:(dk + 1) * 8],
                                        start=(dk == 0), stop=(dk == 7))
                    a(last)
                em("tensor", t3)
                T3c[j2] = cnt["sPE"]
                att = tick("sVE")
                def v3(v, t=att, pe=T3c[j2]):
                    v.wait_ge(sems["sPE"], pe)
                    v.reduce_max(f_red[:, 0:1], pyA[:, 0:8], axis=AX.X)
                    v.drain()
                    t(v.tensor_scalar(f_lg, pyA[:, 0:8], f_red[:, 0:1], None,
                                      op0=ALU.subtract))
                em("vector", v3)
                VCc[j2] = cnt["sVE"]
                att = tick("sACT")
                em("scalar", lambda a, t=att, ve=VCc[j2]: (
                    a.wait_ge(sems["sVE"], ve),
                    t(a.activation(f_lg, f_lg, AF.Exp)))[-1])
                A5c[j2] = cnt["sACT"]
                att = tick("sVE")
                def v4(v, j=j2, t=att, ac=A5c[j2], t4f=T4c.get(j2 - 2, 0)):
                    v.wait_ge(sems["sACT"], ac)
                    if t4f:
                        v.wait_ge(sems["sPE"], t4f)
                    v.reduce_sum(f_red[:, 1:2], f_lg, axis=AX.X)
                    v.drain()
                    v.reciprocal(f_red[:, 2:3], f_red[:, 1:2])
                    v.drain()
                    v.tensor_scalar(f_prb, f_lg, f_red[:, 2:3], None, op0=ALU.mult)
                    v.drain()
                    v.max(f_tp, f_prb)
                    v.drain()
                    v.max_index(f_tpi, f_tp, f_prb)
                    v.drain()
                    v.tensor_copy(g0a[:, j:j + 1], f_tp[:, 0:1])
                    v.tensor_copy(g1a[:, j:j + 1], f_tp[:, 1:2])
                    v.tensor_copy(e0a[:, j:j + 1], f_tpi[:, 0:1])
                    v.tensor_copy(e1a[:, j:j + 1], f_tpi[:, 1:2])
                    v.drain()
                    v.tensor_scalar(f_msk[j % 2][:, 0:8], iotaE[:], e0a[:, j:j + 1],
                                    None, op0=ALU.is_equal)
                    v.tensor_scalar(f_msk[j % 2][:, 8:16], iotaE[:], e1a[:, j:j + 1],
                                    None, op0=ALU.is_equal)
                    t(v.drain())
                em("vector", v4)
                VDc[j2] = cnt["sVE"]
            j4 = r - 2
            if 0 <= j4 < NA:
                att = tick("sPE")
                def t4(t, j=j4, a=att, ve=VDc[j4], af=A6c.get(j4 - 1, 0)):
                    t.wait_ge(sems["sVE"], ve)
                    if af:
                        t.wait_ge(sems["sACT"], af)
                    a(t.transpose(pyA[0:16, 16:144], f_msk[j % 2][:, 0:16], id32[:]))
                em("tensor", t4)
                T4c[j4] = cnt["sPE"]
                att = tick("sACT")
                em("scalar", lambda a, j=j4, t=att, pe=T4c[j4]: (
                    a.wait_ge(sems["sPE"], pe),
                    t(a.copy(maskfm[0:16, j * 128:(j + 1) * 128], pyA[0:16, 16:144])))[-1])
                A6c[j4] = cnt["sACT"]
        F0_ACT = {NA - 1: A6c[NA - 1]}
        AGX_LD = cnt["sLD"]
        wt("gpsimd", "sLD", AGX_LD)
        att = tick("ccXLN")
        em("gpsimd", lambda g, t=att: t(g.collective_compute(
            "AllGather", ALU.bypass, replica_groups=[list(range(NCORE))],
            ins=[agxln_i.ap().opt()], outs=[agxln_o.ap().opt()])))

        if DEBUG:
            dma("sync", dbg["xln"], xln32_d.ap())

        wt("vector", "sACT", F0_ACT[NA - 1])
        att = tick("sVE")
        def ve_f1(v, t=att):
            v.memset(zer16[0:16, :], 0.0)
            v.drain()
            v.tensor_tensor_scan(lcfm[0:16, :], maskfm[0:16, :], zer16[0:16, :],
                                 0.0, op0=ALU.add, op1=ALU.add)
            v.drain()
            t(v.tensor_copy(cntsb[0:16, 0:1], lcfm[0:16, TLOC - 1:TLOC]))
        em("vector", ve_f1)
        ve_t = cnt["sVE"]
        wt("sync", "sVE", ve_t)
        dma("sync", agcnt_i.ap()[:, 0:1], cntsb[0:16, 0:1], slow=True)
        wt("gpsimd", "sLD", cnt["sLD"])
        att = tick("ccCNT")
        em("gpsimd", lambda g, t=att: t(g.collective_compute(
            "AllGather", ALU.bypass, replica_groups=[list(range(NCORE))],
            ins=[agcnt_i.ap().opt()], outs=[agcnt_o.ap().opt()])))
        wt("sync", "ccCNT", 1)
        cnt_ld = dma("sync", cntsb[:], agcnt_o.ap())
        wt("tensor", "sLD", cnt_ld)
        att = tick("sPE")
        em("tensor", lambda t, a=att: a(t.matmul(
            pbig[0:16, 0:1], offsel[:], cntsb[:, 0:1], start=True, stop=True)))
        pe_t = cnt["sPE"]
        wt("scalar", "sPE", pe_t)
        att = tick("sACT")
        em("scalar", lambda a, t=att: t(a.copy(off16[:], pbig[0:16, 0:1])))
        act_t = cnt["sACT"]
        wt("vector", "sACT", act_t)
        att = tick("sVE")
        def ve_pos(v, t=att):
            v.tensor_scalar(off16[:], off16[:], -1.0, None, op0=ALU.add)
            v.drain()
            t(v.tensor_scalar(lcfm[0:16, :], lcfm[0:16, :], off16[:], None,
                              op0=ALU.add))
        em("vector", ve_pos)
        POS_VE = cnt["sVE"]
        if DEBUG:
            wt("sync", "sVE", POS_VE)
            dma("sync", dbg["mfm"], maskfm[0:16, :], semnm="sST")
            dma("sync", dbg["lcf"], lcfm[0:16, :], semnm="sST")
            dma("sync", dbg["o16"], off16[:], semnm="sST")

        wt("gpsimd", "sVE", VE_INIT)
        GD_MAPZ = dma("gpsimd", map_i.ap().rearrange("(p a) w -> p (a w)", p=128),
                      mapz[:])

        F2_ACT = 0
        F2_GD_prev = 0
        for j in range(NA):
            wt("tensor", "sVE", POS_VE)
            if DEBUG and j == 2:
                wt("tensor", "sST", cnt["sST"])
            if F2_ACT:
                wt("tensor", "sACT", F2_ACT)
            att = tick("sPE")
            em("tensor", lambda t, j=j, a=att: a(t.transpose(
                pbig[:, 1536:1536 + 16],
                lcfm[0:16, j * 128:(j + 1) * 128], id32[0:16, 0:16])))
            pe_t = cnt["sPE"]
            wt("scalar", "sPE", pe_t)
            if j >= 1:
                wt("scalar", "sVE", F2_VE_prev)
            att = tick("sACT")
            em("scalar", lambda a, t=att: t(a.copy(postm, pbig[:, 1536:1536 + 16])))
            F2_ACT = cnt["sACT"]
            if DEBUG and j == 1:
                wt("sync", "sACT", F2_ACT)
                dma("sync", dbg["pt1"], postm, semnm="sST")
            wt("vector", "sACT", F2_ACT)
            if F2_GD_prev:
                wt("vector", "sGD", F2_GD_prev)
            att = tick("sVE")
            def ve_f2(v, j=j, t=att):
                v.tensor_scalar(f_msk[0][:, 0:8], iotaE[:], e0a[:, j:j + 1], None,
                                op0=ALU.is_equal)
                v.tensor_scalar(f_msk[0][:, 8:16], iotaE[:], e1a[:, j:j + 1], None,
                                op0=ALU.is_equal)
                v.drain()
                v.tensor_tensor(p_t8, postm[:, 0:8], f_msk[0][:, 0:8], op=ALU.mult)
                v.drain()
                v.reduce_sum(p_c1[:, 0:1], p_t8, axis=AX.X)
                v.drain()
                v.tensor_tensor(p_t8, postm[:, 8:16], f_msk[0][:, 8:16], op=ALU.mult)
                v.drain()
                v.reduce_sum(p_c1[:, 1:2], p_t8, axis=AX.X)
                v.drain()
                v.tensor_scalar(p_c1[:, 2:3], p_c1[:, 0:1], float(CAP) - 0.5, None,
                                op0=ALU.is_lt)
                v.tensor_scalar(p_c1[:, 3:4], p_c1[:, 1:2], float(CAP) - 0.5, None,
                                op0=ALU.is_lt)
                v.drain()
                v.tensor_tensor(g0a[:, j:j + 1], g0a[:, j:j + 1], p_c1[:, 2:3], op=ALU.mult)
                v.tensor_tensor(g1a[:, j:j + 1], g1a[:, j:j + 1], p_c1[:, 3:4], op=ALU.mult)
                v.drain()
                v.tensor_scalar(p_c1[:, 4:5], e0a[:, j:j + 1], float(SLOT), gofs[:, 0:1],
                                op0=ALU.mult, op1=ALU.add)
                v.tensor_scalar(p_c1[:, 5:6], e1a[:, j:j + 1], float(SLOT), gofs[:, 0:1],
                                op0=ALU.mult, op1=ALU.add)
                v.drain()
                v.tensor_tensor(p_c1[:, 4:5], p_c1[:, 4:5], p_c1[:, 0:1], op=ALU.add)
                v.tensor_tensor(p_c1[:, 5:6], p_c1[:, 5:6], p_c1[:, 1:2], op=ALU.add)
                v.drain()
                v.tensor_tensor(p_c1[:, 6:7], p_c1[:, 4:5], p_c1[:, 2:3], op=ALU.mult)
                v.tensor_tensor(p_c1[:, 7:8], p_c1[:, 5:6], p_c1[:, 3:4], op=ALU.mult)
                v.drain()
                v.tensor_copy(gi0a[:, j:j + 1], p_c1[:, 6:7])
                v.tensor_copy(gi1a[:, j:j + 1], p_c1[:, 7:8])
                v.tensor_scalar(p_c1[:, 8:9], p_c1[:, 2:3], -1.0, 1.0,
                                op0=ALU.mult, op1=ALU.add)
                v.tensor_scalar(p_c1[:, 9:10], p_c1[:, 3:4], -1.0, 1.0,
                                op0=ALU.mult, op1=ALU.add)
                v.tensor_scalar(p_c1[:, 10:11], tokid[:, j:j + 1], 30000.0, None,
                                op0=ALU.add)
                v.drain()
                v.tensor_tensor(p_c1[:, 8:9], p_c1[:, 8:9], p_c1[:, 10:11], op=ALU.mult)
                v.tensor_tensor(p_c1[:, 9:10], p_c1[:, 9:10], p_c1[:, 10:11], op=ALU.mult)
                v.drain()
                v.tensor_tensor(p_c1[:, 8:9], p_c1[:, 8:9], p_c1[:, 6:7], op=ALU.add)
                v.tensor_tensor(p_c1[:, 9:10], p_c1[:, 9:10], p_c1[:, 7:8], op=ALU.add)
                v.drain()
                v.tensor_copy(p_u[:, 0:1], p_c1[:, 8:9])
                v.tensor_copy(p_u[:, 1:2], p_c1[:, 9:10])
                v.tensor_scalar(tokrep, iotaE[:], 0.0, tokid[:, j:j + 1],
                                op0=ALU.mult, op1=ALU.add)
                t(v.drain())
            em("vector", ve_f2)
            ve_t = cnt["sVE"]
            F2_VE_prev = cnt["sVE"]
            if DEBUG and j == 1:
                wt("sync", "sVE", ve_t)
                dma("sync", dbg["pc1"], p_c1, semnm="sST")
                dma("sync", dbg["mk1"], f_msk[0], semnm="sST")
                F2_DBG = cnt["sST"]
            wt("gpsimd", "sVE", ve_t)
            if j == 0:
                wt("gpsimd", "sGD", GD_MAPZ)
            for kk in range(2):
                att = tick("sGD", 16)
                em("gpsimd", lambda g, kk=kk, a=att: a(g.indirect_dma_start(
                    out=map_i.ap(),
                    out_offset=bass.IndirectOffsetOnAxis(ap=p_u[:, kk:kk + 1], axis=0),
                    in_=tokrep, in_offset=None,
                    bounds_check=E * SLOT - 1, oob_is_err=False)))
            F2_GD_prev = cnt["sGD"]
        F2_GD = cnt["sGD"]
        wt("gpsimd", "sGD", F2_GD)
        att = tick("ccMAP")
        em("gpsimd", lambda g, t=att: t(g.collective_compute(
            "AllReduce", ALU.add, replica_groups=[list(range(NCORE))],
            ins=[map_i.ap().opt()], outs=[map_o.ap().opt()])))

        if DEBUG:
            wt("sync", "ccMAP", 1)
            dma("sync", dbg["map"], map_o.ap())
            wt("sync", "sVE", cnt["sVE"])
            for col, src in enumerate([g0a, g1a, e0a, e1a]):
                dma("sync", dbg["rt"][:, col * 16:(col + 1) * 16], src[:])
            dma("sync", dbg["rt"][:, 64:80], gi0a[:].bitcast(F32))
            dma("sync", dbg["rt"][:, 80:96], gi1a[:].bitcast(F32))

        wt("sync", "sVE", cnt["sVE"])
        wt("sync", "sGD", GD_WIBF)
        GD_WI_LD = dma("sync", wibf.rearrange("p (dk f) -> p dk f", dk=8),
                       wi_bf_d.ap().rearrange("(dk p) f -> p dk f", p=128))
        wt("gpsimd", "ccMAP", 1)
        for i in range(10):
            att = tick("sGD", 16)
            em("gpsimd", lambda g, i=i, a=att: a(g.indirect_dma_start(
                out=idxf[:, i * 8:(i + 1) * 8], out_offset=None,
                in_=map_o.ap(),
                in_offset=bass.IndirectOffsetOnAxis(ap=mapidx[:, i:i + 1], axis=0))))
        GD_MAPG = cnt["sGD"]
        wt("vector", "sGD", GD_MAPG)
        att = tick("sVE")
        def ve_idx(v, t=att):
            v.tensor_copy(idxu[:], idxf.rearrange("p (i w) -> p i w", w=8)[:, :, 0])
            t(v.drain())
        em("vector", ve_idx)
        VE_IDX = cnt["sVE"]
        wt("gpsimd", "sVE", VE_IDX)
        wt("gpsimd", "ccXLN", 1)
        EIN_ACT = 0
        for i in range(10):
            att = tick("sGD", 16)
            def gp_ein(g, i=i, a=att, pe_need=(cnt["sPE"] if i >= 1 else 0)):
                if pe_need:
                    g.wait_ge(sems["sPE"], pe_need)
                a(g.indirect_dma_start(
                    out=eint, out_offset=None,
                    in_=agxln_o.ap(),
                    in_offset=bass.IndirectOffsetOnAxis(ap=idxu[:, i:i + 1], axis=0)))
            em("gpsimd", gp_ein)
            gd_t = cnt["sGD"]
            wt("tensor", "sGD", gd_t)
            if EIN_ACT:
                wt("tensor", "sACT", EIN_ACT)
            att = tick("sPE")
            def pe_ein(t, a=att):
                last = None
                for dk in range(8):
                    last = t.transpose(pbig_bf[:, dk * 128:dk * 128 + 128],
                                       eint[:, dk * 128:dk * 128 + 128], idbf[:])
                a(last)
            em("tensor", pe_ein)
            pe_t = cnt["sPE"]
            wt("scalar", "sPE", pe_t)
            att = tick("sACT")
            em("scalar", lambda a, i=i, t=att: t(a.copy(
                einfm3[:, :, i * 128:(i + 1) * 128],
                pbig_bf[:, 0:1024].rearrange("p (dk s) -> p dk s", dk=8))))
            EIN_ACT = cnt["sACT"]
        wibf3 = wibf.rearrange("p (dk f) -> p dk f", dk=8)
        hrel32 = hrel.rearrange("p (m c) -> p m c", m=32)
        hrel4v = hrel4.rearrange("p (gm c) -> p gm c", c=128)
        eosb4 = eosb.rearrange("p (ci w) -> p ci w", ci=4)
        EO_LD = 0
        EO_ACT = 0
        MM1_ACT = {}
        woq = 0
        WO_PE = {}
        for g2 in range(2):
            for m in range(32):
                att = tick("sPE")
                def pe_m1(t, g2=g2, m=m, a=att, w_need=GD_WI_LD, e_need=EIN_ACT,
                          prev_act=(cnt["sACT"] if m >= 1 or g2 >= 1 else 0)):
                    t.wait_ge(sems["sLD"], w_need)
                    t.wait_ge(sems["sACT"], e_need)
                    if prev_act:
                        t.wait_ge(sems["sACT"], prev_act)
                    last = None
                    for q0, q1 in [(0, 512), (512, 640)]:
                        for dk in range(8):
                            last = t.matmul(pyA[:, q0:q1],
                                            wibf3[:, dk, m * 128:(m + 1) * 128],
                                            einfm3[:, dk, g2 * 640 + q0:g2 * 640 + q1],
                                            start=(dk == 0), stop=(dk == 7))
                    a(last)
                em("tensor", pe_m1)
                pe_t = cnt["sPE"]
                wt("scalar", "sPE", pe_t)
                att = tick("sACT", 2)
                def act_m1(a, g2=g2, m=m, t=att):
                    a.activation(hrel32[:, m, :], pyA[:, 0:512], AF.Relu)
                    t(a.activation(hrel4v[:, g2 * 32 + m, :], pyA[:, 512:640], AF.Relu))
                em("scalar", act_m1)
            MM1_ACT[g2] = cnt["sACT"]
            HREL4_ACT = cnt["sACT"]
            for m in range(32):
                par = woq % 2
                if woq >= 2:
                    wt("sync", "sPE", WO_PE[woq - 2])
                if woq == 0:
                    wt("sync", "sGD", GD_WOBF)
                wo_t = dma("sync", wobf[par], wo_bf_d.ap()[m * 128:(m + 1) * 128, :])
                att = tick("sPE")
                def pe_m2(t, g2=g2, m=m, par=par, a=att, w=wo_t,
                          h_need=MM1_ACT[g2], prev_evac=(EO_ACT if m == 0 else 0)):
                    t.wait_ge(sems["sLD"], w)
                    t.wait_ge(sems["sACT"], h_need)
                    if prev_evac:
                        t.wait_ge(sems["sACT"], prev_evac)
                    last = None
                    for ci in range(4):
                        dst = (pbig[:, 0:1024], pbig[:, 1024:2048],
                               pyA[:, :], pyB[:, :])[ci]
                        for q in range(2):
                            last = t.matmul(
                                dst[:, q * 512:(q + 1) * 512],
                                hrel32[:, m, ci * 128:(ci + 1) * 128],
                                wobf[par][:, q * 512:(q + 1) * 512],
                                start=(m == 0), stop=(m == 31))
                    a(last)
                em("tensor", pe_m2)
                WO_PE[woq] = cnt["sPE"]
                woq += 1
            pe_t = cnt["sPE"]
            wt("scalar", "sPE", pe_t)
            if EO_LD:
                wt("scalar", "sLD", EO_LD)
            att = tick("sACT", 4)
            def act_eo(a, t=att):
                srcs = (pbig[:, 0:1024], pbig[:, 1024:2048], pyA[:, :], pyB[:, :])
                last = None
                for ci in range(4):
                    last = a.copy(eosb4[:, ci, :], srcs[ci])
                t(last)
            em("scalar", act_eo)
            EO_ACT = cnt["sACT"]
            wt("sync", "sACT", EO_ACT)
            for ci in range(4):
                EO_LD = dma("sync",
                            ageo_i.ap()[g2 * CAP + ci * 128: g2 * CAP + ci * 128 + 128, :],
                            eosb4[:, ci, :])
        for m in range(32):
            par = woq % 2
            wt("sync", "sPE", WO_PE[woq - 2])
            wo_t = dma("sync", wobf[par], wo_bf_d.ap()[m * 128:(m + 1) * 128, :])
            att = tick("sPE")
            def pe_m2c(t, m=m, par=par, a=att, w=wo_t,
                       h_need=MM1_ACT[1], prev_evac=(EO_ACT if m == 0 else 0)):
                t.wait_ge(sems["sLD"], w)
                t.wait_ge(sems["sACT"], h_need)
                if prev_evac:
                    t.wait_ge(sems["sACT"], prev_evac)
                last = None
                for gi2 in range(2):
                    lh = hrel4v[:, gi2 * 32 + m, :]
                    dst = pbig[:, 0:1024] if gi2 == 0 else pbig[:, 1024:2048]
                    for q in range(2):
                        last = t.matmul(dst[:, q * 512:(q + 1) * 512], lh,
                                        wobf[par][:, q * 512:(q + 1) * 512],
                                        start=(m == 0), stop=(m == 31))
                a(last)
            em("tensor", pe_m2c)
            WO_PE[woq] = cnt["sPE"]
            woq += 1
        pe_t = cnt["sPE"]
        wt("scalar", "sPE", pe_t)
        wt("scalar", "sLD", EO_LD)
        att = tick("sACT", 2)
        def act_eoc(a, t=att):
            a.copy(eosb4[:, 0, :], pbig[:, 0:1024])
            t(a.copy(eosb4[:, 1, :], pbig[:, 1024:2048]))
        em("scalar", act_eoc)
        EO_ACT = cnt["sACT"]
        wt("sync", "sACT", EO_ACT)
        for gi2 in range(2):
            EO_LD = dma("sync",
                        ageo_i.ap()[gi2 * CAP + 512: gi2 * CAP + 640, :],
                        eosb4[:, gi2, :])
        wt("gpsimd", "sLD", EO_LD)
        att = tick("ccEO")
        em("gpsimd", lambda g, t=att: t(g.collective_compute(
            "AllGather", ALU.bypass, replica_groups=[list(range(NCORE))],
            ins=[ageo_i.ap().opt()], outs=[ageo_o.ap().opt()])))

        GDc = {}; XLc = {}; V5c = {}; A7c = {}; V6c = {}; STc = {}
        wt("gpsimd", "ccEO", 1)
        for r in range(NA + 1):
            j = r
            if j < NA:
                p2 = j % 2
                if j >= 2:
                    wt("gpsimd", "sVE", V5c[j - 2])
                att = tick("sGD", 16)
                em("gpsimd", lambda g, j=j, p2=p2, a=att: a(g.indirect_dma_start(
                    out=cmb_g0[p2], out_offset=None, in_=ageo_o.ap(),
                    in_offset=bass.IndirectOffsetOnAxis(ap=gi0a[:, j:j + 1], axis=0))))
                att = tick("sGD", 16)
                em("gpsimd", lambda g, j=j, p2=p2, a=att: a(g.indirect_dma_start(
                    out=cmb_g1[p2], out_offset=None, in_=ageo_o.ap(),
                    in_offset=bass.IndirectOffsetOnAxis(ap=gi1a[:, j:j + 1], axis=0))))
                GDc[j] = cnt["sGD"]
                if j >= 2:
                    wt("sync", "sVE", V5c[j - 2])
                XLc[j] = dma("sync", cmb_xl[p2], xln32_d.ap()[j * 128:(j + 1) * 128, :])
                att = tick("sVE")
                def v5(v, j=j, p2=p2, t=att, gd=GDc[j], ld=XLc[j]):
                    v.wait_ge(sems["sGD"], gd)
                    v.wait_ge(sems["sLD"], ld)
                    v.tensor_scalar(cmb_t1, cmb_g1[p2], g1a[:, j:j + 1], None,
                                    op0=ALU.mult)
                    v.drain()
                    v.scalar_tensor_tensor(cmb_yf[p2], cmb_g0[p2], g0a[:, j:j + 1],
                                           cmb_t1, op0=ALU.mult, op1=ALU.add)
                    v.drain()
                    v.tensor_tensor(cmb_yf[p2], cmb_yf[p2], cmb_xl[p2], op=ALU.add)
                    v.drain()
                    v.bn_stats(cmb_bns[:, 0:6], cmb_yf[p2][:, 0:512])
                    v.bn_stats(cmb_bns[:, 6:12], cmb_yf[p2][:, 512:1024])
                    v.drain()
                    v.bn_aggr(cmb_bna[p2], cmb_bns[:, 0:12])
                    v.drain()
                    v.tensor_scalar(cmb_ms[p2][:, 2:3], cmb_bna[p2][:, 1:2], EPS,
                                    None, op0=ALU.add)
                    t(v.drain())
                em("vector", v5)
                V5c[j] = cnt["sVE"]
            j6 = r - 1
            if 0 <= j6 < NA:
                p2 = j6 % 2
                att = tick("sACT")
                em("scalar", lambda a, p2=p2, t=att, ve=V5c[j6]: (
                    a.wait_ge(sems["sVE"], ve),
                    t(a.activation(cmb_ms[p2][:, 0:1], cmb_ms[p2][:, 2:3], AF.Sqrt)))[-1])
                A7c[j6] = cnt["sACT"]
                att = tick("sVE")
                def v6(v, j=j6, p2=p2, t=att, ac=A7c[j6], stf=STc.get(j6 - 2, 0)):
                    v.wait_ge(sems["sACT"], ac)
                    if stf:
                        v.wait_ge(sems["sST"], stf)
                    v.reciprocal(cmb_ms[p2][:, 1:2], cmb_ms[p2][:, 0:1])
                    v.drain()
                    t(v.tensor_scalar(cmb_o[p2], cmb_yf[p2], cmb_bna[p2][:, 0:1],
                                      cmb_ms[p2][:, 1:2],
                                      op0=ALU.subtract, op1=ALU.mult))
                em("vector", v6)
                V6c[j6] = cnt["sVE"]
                wt("sync", "sVE", V6c[j6])
                att = tick("sST", 16)
                em("sync", lambda e, j=j6, p2=p2, a=att: a(e.dma_start(
                    out=out_t[j * 128:(j + 1) * 128, :], in_=cmb_o[p2])))
                STc[j6] = cnt["sST"]
        wt("sync", "sST", cnt["sST"])

        @block.sync
        def _(e):
            for fn in prog["sync"]:
                fn(e)

        @block.gpsimd
        def _(e):
            for fn in prog["gpsimd"]:
                fn(e)

        @block.vector
        def _(e):
            for fn in prog["vector"]:
                fn(e)

        @block.scalar
        def _(e):
            for fn in prog["scalar"]:
                fn(e)

        @block.tensor
        def _(e):
            for fn in prog["tensor"]:
                fn(e)

    return nc



def _prep_core_inputs(c, xs, W_dt, b_dt, W_B, W_C, A_log, Dv, w_r, wi, wo):
    f32 = np.float32
    d = {}
    d["x"] = np.ascontiguousarray(xs[c * TLOC:(c + 1) * TLOC]).astype(f32, copy=False)
    d["wdt"] = W_dt
    bdt_t = np.zeros((128, 8), f32)
    for ee in range(8):
        bdt_t[:, ee] = b_dt[ee * 128:(ee + 1) * 128]
    d["bdt"] = bdt_t
    d["wbc"] = np.concatenate([W_B, W_C], axis=0).astype(f32, copy=False)
    adiag = np.zeros((128, 2048), f32)
    sel01 = np.zeros((128, 2048), f32)
    dl = np.arange(8)
    n = np.arange(NST)
    for k in range(16):
        rows = (8 * k + dl)[:, None]
        cols = (k * 128 + dl[:, None] * 16 + n[None, :])
        adiag[rows, cols] = A_log[128 * c + 8 * k + dl, :]
        sel01[rows, cols] = 1.0
    d["adiag"] = adiag
    d["sel01"] = sel01
    selbc = np.zeros((32, 256), f32)
    p = np.arange(128)
    selbc[p % 16, p] = 1.0
    selbc[16 + (p % 16), 128 + p] = 1.0
    d["selbc"] = selbc
    selr = np.zeros((128, 8), f32)
    selr[p, p // 16] = 1.0
    d["selr"] = selr
    d["id32"] = np.eye(128, dtype=f32)
    d["idbf"] = np.eye(128, dtype=ml_dtypes.bfloat16)
    d["drep"] = np.broadcast_to(Dv, (128, D)).copy()
    d["wr"] = w_r
    d["wi"] = wi[c]
    d["wo"] = wo[c]
    offsel = np.zeros((128, 16), f32)
    for c1 in range(8):
        if c1 // 4 != c // 4:
            continue
        for k1 in range(2):
            for k in range(2):
                if (k1 < k) or (k1 == k and c1 < c):
                    for e in range(8):
                        offsel[c1 * 16 + k1 * 8 + e, k * 8 + e] = 1.0
    d["offsel"] = offsel
    tok = np.zeros((128, 16), f32)
    for j in range(16):
        tok[:, j] = c * TLOC + j * 128 + np.arange(128)
    d["tokid"] = tok
    d["gofs"] = np.full((128, 1), (c // 4) * CAP, f32)
    mi = np.zeros((128, 10), np.uint32)
    for i in range(10):
        mi[:, i] = c * SLOT + i * 128 + np.arange(128)
    d["mapidx"] = mi
    d["iotaE"] = np.broadcast_to(np.arange(8, dtype=f32), (128, 8)).copy()
    return d


def kernel(L0_attn_x, L0_attn_W_dt, L0_attn_b_dt, L0_attn_W_B, L0_attn_W_C,
           L0_attn_A_log, L0_attn_D, L0_ffn_w_router, L0_ffn_wi, L0_ffn_wo):
    xs = np.ascontiguousarray(np.asarray(L0_attn_x, np.float32).reshape(TTOT, D))
    W_dt = np.ascontiguousarray(np.asarray(L0_attn_W_dt, np.float32))
    b_dt = np.asarray(L0_attn_b_dt, np.float32)
    W_B = np.asarray(L0_attn_W_B, np.float32)
    W_C = np.asarray(L0_attn_W_C, np.float32)
    A_log = np.asarray(L0_attn_A_log, np.float32)
    Dv = np.asarray(L0_attn_D, np.float32)
    w_r = np.ascontiguousarray(np.asarray(L0_ffn_w_router, np.float32))
    wi = np.ascontiguousarray(np.asarray(L0_ffn_wi, np.float32))
    wo = np.ascontiguousarray(np.asarray(L0_ffn_wo, np.float32))

    nc = build_program()
    in_maps = [_prep_core_inputs(c, xs, W_dt, b_dt, W_B, W_C, A_log, Dv, w_r, wi, wo)
               for c in range(NCORE)]
    res = run_bass_kernel_spmd(nc, in_maps, core_ids=list(range(NCORE)))
    out = np.concatenate([res.results[c]["out"] for c in range(NCORE)], axis=0)
    return out.reshape(2, TTOT // 2, D)
